# version 20
# baseline (speedup 1.0000x reference)
import numpy as np
import ml_dtypes
import concourse.bass as bass
import concourse.mybir as mybir
from concourse.bass_utils import run_bass_kernel_spmd

F32 = mybir.dt.float32
BF16 = mybir.dt.bfloat16
AF = mybir.ActivationFunctionType
ALU = mybir.AluOpType
AX = mybir.AxisListType
bf16 = ml_dtypes.bfloat16

B = 2
S = 8192
D = 1024
NCORE = 8
TQ = 2048
NPAIR = 8
NKT = S // 128
NQB = TQ // 512
NTC = S // 512
NT1 = S // 128
NT3 = TQ // 128
LOG2E = float(np.log2(np.e))
LN2 = float(np.log(2.0))


DBG_PHASES = ("ph0", "ph1", "pairs", "ph3")
DBG_NPAIR = NPAIR
DBG_PROJ = ("k", "v", "q", "norms", "rope", "m0")
DBG_DUMP = False
DUMMY_N = 0
DBG_PH1_BARRIER = False
DBG_PH1_GUARD = False
DBG_PH1_SELFSYNC = False


def build_program():
    nc = bass.Bass("TRN2", target_bir_lowering=False)

    x_in = nc.dram_tensor("x", [S, D], BF16, kind="ExternalInput").ap()
    wqT = nc.dram_tensor("wqT", [D, D], BF16, kind="ExternalInput").ap()
    wkT = nc.dram_tensor("wkT", [D, D], BF16, kind="ExternalInput").ap()
    wvT = nc.dram_tensor("wvT", [D, D], BF16, kind="ExternalInput").ap()
    woT = nc.dram_tensor("woT", [D, D], BF16, kind="ExternalInput").ap()
    bqT = nc.dram_tensor("bqT", [128, 8], BF16, kind="ExternalInput").ap()
    bkT = nc.dram_tensor("bkT", [128, 8], BF16, kind="ExternalInput").ap()
    bv_r = nc.dram_tensor("bv", [1, D], BF16, kind="ExternalInput").ap()
    bo_r = nc.dram_tensor("bo", [1, D], BF16, kind="ExternalInput").ap()
    fbias = nc.dram_tensor("fbias", [1, D], F32, kind="ExternalInput").ap()
    g1r = nc.dram_tensor("g1r", [32, 4, 32], F32, kind="ExternalInput").ap()
    g1i = nc.dram_tensor("g1i", [32, 4, 32], F32, kind="ExternalInput").ap()
    g2r = nc.dram_tensor("g2r", [16, 4, 16], F32, kind="ExternalInput").ap()
    g2i = nc.dram_tensor("g2i", [16, 4, 16], F32, kind="ExternalInput").ap()
    cosT = nc.dram_tensor("cosT", [128, S], BF16, kind="ExternalInput").ap()
    sinT = nc.dram_tensor("sinT", [128, S], BF16, kind="ExternalInput").ap()
    identT = nc.dram_tensor("identT", [128, 128], BF16, kind="ExternalInput").ap()
    rotT = nc.dram_tensor("rotT", [128, 128], BF16, kind="ExternalInput").ap()
    hsel = nc.dram_tensor("hsel", [128, 33], BF16, kind="ExternalInput").ap()
    out_d = nc.dram_tensor("out", [TQ, D], F32, kind="ExternalOutput").ap()
    if DBG_DUMP:
        dmp_ln = nc.dram_tensor("dmp_ln", [D, S], BF16, kind="ExternalOutput").ap()
        dmp_ks = nc.dram_tensor("dmp_ks", [128, 2 * S], BF16, kind="ExternalOutput").ap()
        dmp_vs = nc.dram_tensor("dmp_vs", [128, 2 * S], BF16, kind="ExternalOutput").ap()
        dmp_qs = nc.dram_tensor("dmp_qs", [128, 2 * TQ], BF16, kind="ExternalOutput").ap()
        dmp_at = nc.dram_tensor("dmp_at", [128, NPAIR * TQ], BF16, kind="ExternalOutput").ap()
        dmp_bp = nc.dram_tensor("dmp_bp", [128, 2], F32, kind="ExternalOutput").ap()
        dmp_wc = nc.dram_tensor("dmp_wc", [128, 2 * S], BF16, kind="ExternalOutput").ap()
        dmp_m0 = nc.dram_tensor("dmp_m0", [128, 16], F32, kind="ExternalOutput").ap()
        dmp_st = nc.dram_tensor("dmp_st", [128, 512], F32, kind="ExternalOutput").ap()
        dmp_ln2 = nc.dram_tensor("dmp_ln2", [D, S], BF16, kind="ExternalOutput").ap()
    ln1T_d = nc.dram_tensor("ln1T_d", [D, S], BF16)
    wcat_d = nc.dram_tensor("wcat_d", [2, D, D], BF16)

    from contextlib import ExitStack
    with ExitStack() as _st:
        ec = _st.enter_context
        attnT = ec(nc.sbuf_tensor("attnT", [128, NPAIR * TQ], BF16))
        big4 = ec(nc.sbuf_tensor("big4", [128, 2 * S], BF16))
        vs = ec(nc.sbuf_tensor("vs", [128, 2 * S], BF16))
        lnt = ec(nc.sbuf_tensor("lnt", [128, 2 * 4096], BF16))
        qs = ec(nc.sbuf_tensor("qs", [128, 2 * TQ], BF16))
        pp = ec(nc.sbuf_tensor("pp", [128, 4 * 1024], BF16))
        cossin = ec(nc.sbuf_tensor("cossin", [128, 2 * 1024], BF16))
        wpair = ec(nc.sbuf_tensor("wpair", [128, 2 * 3072], BF16))
        warena = ec(nc.sbuf_tensor("warena", [128, 8192], BF16))
        stage = ec(nc.sbuf_tensor("stage", [128, 4096], BF16))
        av32 = ec(nc.sbuf_tensor("av32", [128, 512], F32))
        v32 = ec(nc.sbuf_tensor("v32", [128, 2048], F32))
        outb = ec(nc.sbuf_tensor("outb", [128, 2048], F32))
        r32 = ec(nc.sbuf_tensor("r32", [128, 512], F32))
        ones = ec(nc.sbuf_tensor("ones", [128, 128], BF16))
        ident_s = ec(nc.sbuf_tensor("ident_s", [128, 128], BF16))
        rot_s = ec(nc.sbuf_tensor("rot_s", [128, 128], BF16))
        hsel_s = ec(nc.sbuf_tensor("hsel_s", [128, 33], BF16))
        bq32 = ec(nc.sbuf_tensor("bq32", [128, 8], F32))
        bk32 = ec(nc.sbuf_tensor("bk32", [128, 8], F32))
        bqbk = ec(nc.sbuf_tensor("bqbk", [128, 16], BF16))
        bv_s = ec(nc.sbuf_tensor("bv_s", [1, D], BF16))
        bo_s = ec(nc.sbuf_tensor("bo_s", [1, D], BF16))
        fbh = ec(nc.sbuf_tensor("fbh", [1, D], BF16))
        fbl = ec(nc.sbuf_tensor("fbl", [1, D], BF16))
        fb32 = ec(nc.sbuf_tensor("fb32", [1, D], F32))
        eps_t = ec(nc.sbuf_tensor("eps_t", [128, 1], F32))
        stats = ec(nc.sbuf_tensor("stats", [128, 64], F32))
        m0buf = ec(nc.sbuf_tensor("m0buf", [128, 16], F32))
        m0bf = ec(nc.sbuf_tensor("m0bf", [1, 2], BF16))
        biasp = ec(nc.sbuf_tensor("biasp", [128, 2], F32))
        astore = ec(nc.sbuf_tensor("astore", [128, 1024], F32))
        bstore = ec(nc.sbuf_tensor("bstore", [128, 128], F32))
        sttrace = ec(nc.sbuf_tensor("sttrace", [128, 512], F32))
        ps_ab = ec(nc.psum_tensor("ps_ab", [128, 2048], F32))
        ps_a = ps_ab.ap()[:, 0:1024]
        ps_b = ps_ab.ap()[:, 1024:2048]
        ps_av = ec(nc.psum_tensor("ps_av", [128, 512], F32))
        ps_den = ec(nc.psum_tensor("ps_den", [128, 512], F32))
        ps_p = ec(nc.psum_tensor("ps_p", [128, 512], F32))
        ps_m = ec(nc.psum_tensor("ps_m", [128, 512], F32))
        sL = ec(nc.semaphore("sL"))
        sG = ec(nc.semaphore("sG"))
        sP = ec(nc.semaphore("sP"))
        sA = ec(nc.semaphore("sA"))
        sV = ec(nc.semaphore("sV"))
        sB = ec(nc.semaphore("sB"))
        block = ec(nc.Block())
        ENG = ("sync", "gpsimd", "tensor", "scalar", "vector")
        ops = {e: [] for e in ENG}
        cnt = {"sL": 0, "sG": 0, "sP": 0, "sA": 0, "sV": 0}
        sem = {"sL": sL, "sG": sG, "sP": sP, "sA": sA, "sV": sV}

        def emit(eng, fn):
            ops[eng].append(fn)

        def dma(eng, out, in_, s="sL"):
            cnt[s] += 16
            tok = cnt[s]
            sm = sem[s]
            emit(eng, lambda e, o=out, i=in_, m=sm: e.dma_start(out=o, in_=i).then_inc(m, 16))
            return tok

        def wait(eng, s, tok):
            if tok and tok > 0:
                sm = sem[s]
                emit(eng, lambda e, m=sm, t=tok: e.wait_ge(m, t))

        def runw(eng, fn, s="sV"):
            tok = run(eng, fn, s=s)
            wait(eng, s, tok)
            return tok

        def run(eng, fn, s=None):
            if s is None:
                emit(eng, fn)
                return None
            cnt[s] += 1
            tok = cnt[s]
            sm = sem[s]
            emit(eng, lambda e, f=fn, m=sm: f(e).then_inc(m, 1))
            return tok

        nbar = [0]

        def barrier():
            nbar[0] += 1
            target = 5 * nbar[0]
            emit("sync", lambda e, t=cnt["sL"]: e.wait_ge(sL, t) if t else None)
            emit("gpsimd", lambda e, t=cnt["sG"]: e.wait_ge(sG, t) if t else None)
            for eng in ENG:
                emit(eng, lambda e: e.sem_inc(sB, 1))
                emit(eng, lambda e, t=target: e.wait_ge(sB, t))

        t_id = dma("sync", ident_s[:], identT)
        t_rot = dma("sync", rot_s[:], rotT)
        t_rot = dma("sync", hsel_s[:], hsel)
        dma("sync", bqbk[:, 0:8], bqT)
        t_bq = dma("sync", bqbk[:, 8:16], bkT)
        dma("sync", bv_s[0:1, :], bv_r)
        dma("sync", bo_s[0:1, :], bo_r)
        t_fb = dma("sync", fb32[:], fbias)
        t_ab = 0
        for ri, g1 in ((0, g1r), (1, g1i)):
            for ch in range(4):
                for g in range(16):
                    dst = astore.ap()[g:128:16,
                                      (ri * 4 + ch) * 128:(ri * 4 + ch) * 128 + 128] \
                        .rearrange("p (c b) -> p c b", b=32)
                    t_ab = dma("sync", dst, g1[8 * ch:8 * ch + 8, :, :])
        for ri, g2 in ((0, g2r), (1, g2i)):
            for d8 in range(8):
                dst = bstore.ap()[16 * d8:16 * d8 + 16, ri * 64:(ri + 1) * 64] \
                    .rearrange("p (c e) -> p c e", e=16)
                t_ab = dma("sync", dst, g2[:, :, :])
        run("vector", lambda e: e.memset(ones[:], 1.0))
        run("vector", lambda e: e.memset(eps_t[:], 1e-5))
        run("vector", lambda e: e.memset(m0buf[:], 0.0))
        wait("vector", "sL", t_bq)
        run("vector", lambda e: e.tensor_copy(bq32[:], bqbk[:, 0:8]))
        run("vector", lambda e: e.tensor_copy(bk32[:], bqbk[:, 8:16]))
        wait("vector", "sL", t_fb)
        run("vector", lambda e: e.tensor_copy(fbh[0:1, :], fb32[:]))
        t_fbl = run("vector", lambda e: e.tensor_tensor(fbl[0:1, :], fb32[:], fbh[0:1, :], ALU.subtract), s="sV")

        xw = warena.ap().bitcast(F32)
        wait("vector", "sL", t_ab)
        for ri in range(2):
            for ch in range(4):
                xt = xw[:, (ri * 4 + ch) * 512:(ri * 4 + ch) * 512 + 512] \
                    .rearrange("p (b e) -> p b e", e=16)
                for c in range(4):
                    a_v = astore.ap()[:, (ri * 4 + ch) * 128 + c * 32:(ri * 4 + ch) * 128 + (c + 1) * 32]
                    a_b = a_v[:, :, None].broadcast_to([128, 32, 16])
                    b_v = bstore.ap()[:, ri * 64 + c * 16:ri * 64 + (c + 1) * 16]
                    b_b = b_v[:, None, :].broadcast_to([128, 32, 16])
                    if c == 0:
                        run("vector", lambda e, o=xt, a=a_b, b=b_b: e.tensor_tensor(o, a, b, ALU.mult))
                    else:
                        jt = v32.ap()[:, 0:512].rearrange("p (b e) -> p b e", e=16)
                        run("vector", lambda e, o=jt, a=a_b, b=b_b: e.tensor_tensor(o, a, b, ALU.mult))
                        run("vector", lambda e, o=xt, j=jt: e.tensor_tensor(o, o, j, ALU.add))
        t_wc = 0
        for dc in range(8):
            for half in range(2):
                ch = dc % 4
                if dc < 4:
                    ri, neg = (0, False) if half == 0 else (1, False)
                else:
                    ri, neg = (1, True) if half == 0 else (0, False)
                src = xw[:, (ri * 4 + ch) * 512:(ri * 4 + ch) * 512 + 512]
                hi = big4.ap()[:, dc * 1024 + half * 512:dc * 1024 + half * 512 + 512]
                lo = big4.ap()[:, 8192 + dc * 1024 + half * 512:8192 + dc * 1024 + half * 512 + 512]
                if neg:
                    run("vector", lambda e, s=src: e.tensor_scalar_mul(s, s, -1.0))
                run("vector", lambda e, h=hi, s=src: e.tensor_copy(h, s))
                t_wc = run("vector", lambda e, l=lo, s=src, h=hi: e.tensor_tensor(l, s, h, ALU.subtract), s="sV")
        wait("gpsimd", "sV", t_wc)
        t_wcd = 0
        for hl in range(2):
            for dc in range(8):
                t_wcd = dma("gpsimd", wcat_d[hl, dc * 128:(dc + 1) * 128, :],
                            big4.ap()[:, hl * 8192 + dc * 1024:hl * 8192 + (dc + 1) * 1024], s="sG")

        tok_sp = [0, 0, 0, 0]
        last_apply = [0, 0, 0, 0]
        tr_cp = [0, 0]
        tr_done_a = [0, 0, 0, 0]
        tr_done_v = [0, 0, 0, 0]
        p1_tx = [0, 0, 0, 0]
        p1_tv = [0, 0, 0, 0]
        p1_ta = [0, 0, 0, 0]
        p1_tn = [0, 0, 0, 0]
        sqscr = v32.ap().bitcast(BF16)[:, 0:1024]

        def p1_xa(t):
            par = t & 3
            arena = stage if par < 2 else lnt
            q_ = par & 1
            return arena.ap()[:, q_ * 1024:(q_ + 1) * 1024]

        def p1_la(t):
            par = t & 3
            arena = stage if par < 2 else lnt
            q_ = par & 1
            return arena.ap()[:, 2048 + q_ * 1024:2048 + (q_ + 1) * 1024]

        def p1_st(t, j):
            par = t & 3
            return stats.ap()[:, par * 8 + j:par * 8 + j + 1]

        def st_load(t):
            par = t & 3
            wait("gpsimd", "sA", last_apply[par])
            p1_tx[par] = dma("gpsimd", p1_xa(t), x_in[t * 128:(t + 1) * 128, :], s="sG")

        def st_stats(t):
            par = t & 3
            xa = p1_xa(t)
            sm, sq, mn, vr = p1_st(t, 0), p1_st(t, 1), p1_st(t, 2), p1_st(t, 3)
            wait("vector", "sG", p1_tx[par])
            run("vector", lambda e, o=sm, i=xa: e.reduce_sum(o, i, axis=AX.X))
            tst = run("vector", lambda e, i=xa, q=sq: e.scalar_tensor_tensor(sqscr, i, 1.0, i, ALU.mult, ALU.mult, accum_out=q), s="sV")
            wait("vector", "sV", tst)
            runw("vector", lambda e, m=mn, s_=sm: e.tensor_scalar_mul(m, s_, 1.0 / 1024.0))
            runw("vector", lambda e, v=vr, q=sq: e.tensor_scalar_mul(v, q, 1.0 / 1024.0))
            runw("vector", lambda e, m=mn: e.tensor_tensor(r32.ap()[:, 0:1], m, m, ALU.mult))
            p1_tv[par] = runw("vector", lambda e, v=vr: e.tensor_tensor(v, v, r32.ap()[:, 0:1], ALU.subtract))
            if DBG_DUMP:
                run("vector", lambda e, t=t, s_=stats.ap()[:, par * 8:par * 8 + 6]:
                    e.tensor_copy(sttrace.ap()[:, t * 8:t * 8 + 6], s_))

        def st_lnexp(t):
            par = t & 3
            vr, rs = p1_st(t, 3), p1_st(t, 4)
            wait("scalar", "sV", p1_tv[par])
            _tl = run("scalar", lambda e, o=rs, v=vr: e.activation(o, v, AF.Ln, bias=eps_t.ap()[:, 0:1]), s="sA")
            wait("scalar", "sA", _tl)
            p1_ta[par] = run("scalar", lambda e, o=rs: e.activation(o, o, AF.Exp, scale=-0.5), s="sA")
            wait("scalar", "sA", p1_ta[par])

        def st_tn(t):
            par = t & 3
            mn, rs, nm = p1_st(t, 2), p1_st(t, 4), p1_st(t, 5)
            wait("vector", "sA", p1_ta[par])
            p1_tn[par] = runw("vector", lambda e, n=nm, m=mn, r=rs: e.tensor_scalar(n, m, r, -1.0, ALU.mult, ALU.mult))

        def st_apply(t):
            par = t & 3
            rs, nm = p1_st(t, 4), p1_st(t, 5)
            wait("scalar", "sV", p1_tn[par])
            wait("scalar", "sG", tok_sp[par])
            last_apply[par] = run("scalar", lambda e, o=p1_la(t), i=p1_xa(t), r=rs, n=nm:
                                  e.activation(o, i, AF.Identity, bias=n, scale=r), s="sA")

        def st_tr(t):
            par = t & 3
            la = p1_la(t)
            trbank = (ps_p.ap().bitcast(BF16)[:, 0:128], ps_m.ap().bitcast(BF16)[:, 0:128])
            wait("tensor", "sA", last_apply[par])
            if t == 0:
                wait("tensor", "sL", t_id)
            for dc in range(8):
                b = dc & 1
                wait("tensor", "sA" if b == 0 else "sV", tr_cp[b])
                ttr = run("tensor", lambda e, o=trbank[b], i=la[:, dc * 128:(dc + 1) * 128]:
                          e.transpose(o, i, ident_s[:]), s="sP")
                if b == 0:
                    wait("scalar", "sP", ttr)
                    tr_cp[0] = run("scalar", lambda e, o=la[:, dc * 128:(dc + 1) * 128], i=trbank[0]:
                                   e.copy(o, i), s="sA")
                else:
                    wait("vector", "sP", ttr)
                    tr_cp[1] = run("vector", lambda e, o=la[:, dc * 128:(dc + 1) * 128], i=trbank[1]:
                                   e.tensor_copy(o, i), s="sV")
            tr_done_a[par] = tr_cp[0]
            tr_done_v[par] = tr_cp[1]

        def st_spill(t):
            par = t & 3
            wait("gpsimd", "sA", tr_done_a[par])
            wait("gpsimd", "sV", tr_done_v[par])
            dst = ln1T_d.ap().rearrange("(dc p) t -> p dc t", p=128)[:, :, t * 128:(t + 1) * 128]
            tok_sp[par] = dma("gpsimd", dst, p1_la(t).rearrange("p (dc t) -> p dc t", t=128), s="sG")

        P1_PIPE = ((6, st_spill), (4, st_apply), (2, st_lnexp),
                   (1, st_stats), (3, st_tn), (5, st_tr), (0, st_load))
        if "ph1" in DBG_PHASES:
            for step in range(NT1 + 6):
                for k, fn in P1_PIPE:
                    t = step - k
                    if 0 <= t < NT1:
                        fn(t)

        barrier()
        if DBG_DUMP:
            dd0 = dma("gpsimd", dmp_ln2[:, :], ln1T_d[:, :], s="sG")
            wait("gpsimd", "sG", dd0)
            barrier()

        kready = [[0] * NTC for _ in range(NPAIR)]
        vready = [[0] * NTC for _ in range(NPAIR)]
        qready = [0] * NPAIR
        bready = [0] * NPAIR
        ks_free = [0, 0]
        vs_free = [0, 0]
        qs_free = [0, 0]
        lnt_free = [0, 0]
        cs_free = [0, 0]
        wp_free = [0, 0]
        biasp_free = [0, 0]
        PS = {"psp": 0, "psm": 0}
        qrope = [0] * NPAIR

        def lnslice(tpar, dc):
            return lnt.ap()[:, tpar * 4096 + dc * 512:tpar * 4096 + (dc + 1) * 512]

        def wslice(par, j, dc):
            o = par * 3072 + j * 1024 + dc * 128
            return wpair.ap()[:, o:o + 128]

        def emit_proj_unit(p, tc, bites=None):
            par = p & 1
            tpar = tc & 1
            kslot = big4.ap()[:, par * S + tc * 512:par * S + (tc + 1) * 512]
            vslot = vs.ap()[:, par * S + tc * 512:par * S + (tc + 1) * 512]
            qslot = qs.ap()[:, par * TQ + tc * 512:par * TQ + (tc + 1) * 512] if tc < 4 else None
            U = {}

            def sec(fn):
                if bites is None:
                    fn()
                else:
                    bites.append(fn)

            def s_dma():
                if tc == 0:
                    wait("sync", "sP", wp_free[par])
                    for j, w in enumerate((wkT, wqT, wvT)):
                        for dc in range(8):
                            PS[f"w{par}"] = dma("sync", wslice(par, j, dc),
                                                w[dc * 128:(dc + 1) * 128, p * 128:(p + 1) * 128])
                wait("sync", "sP", lnt_free[tpar])
                for dc in range(8):
                    PS[f"ln{tpar}"] = dma("sync", lnslice(tpar, dc),
                                          ln1T_d[dc * 128:(dc + 1) * 128, tc * 512:(tc + 1) * 512])
                wait("sync", "sV", cs_free[tpar])
                dma("sync", cossin.ap()[:, tpar * 1024:tpar * 1024 + 512], cosT[:, tc * 512:(tc + 1) * 512])
                U["t_cs"] = dma("sync", cossin.ap()[:, tpar * 1024 + 512:tpar * 1024 + 1024],
                                sinT[:, tc * 512:(tc + 1) * 512])
                U["t_ln"] = PS[f"ln{tpar}"]
                U["t_w"] = PS[f"w{par}"]
            sec(s_dma)

            def s_k():
                wait("tensor", "sL", max(U["t_ln"], U["t_w"]))
                if tc == 0:
                    wait("tensor", "sP", ks_free[par])
                    wait("tensor", "sP", vs_free[par])
                    wait("tensor", "sP", qs_free[par])
                wait("tensor", "sV", PS["psp"])
                tkmm = None
                for dc in range(8):
                    tkmm = run("tensor", lambda e, a=wslice(par, 0, dc), b=lnslice(tpar, dc), st=(dc == 0), sp=(dc == 7):
                               e.matmul(ps_p[:], a, b, start=st, stop=sp), s="sP" if dc == 7 else None)
                wait("vector", "sP", tkmm)
                U["tkev"] = run("vector", lambda e, o=kslot: e.tensor_scalar(o, ps_p[:], bk32.ap()[:, p:p + 1],
                                                                            None, ALU.add), s="sV")
                PS["psp"] = U["tkev"]
            sec(s_k)

            def s_v(g0):
                vev = U["vev"]
                for g in (g0, g0 + 1):
                    bank = ps_m if (g & 1) == 0 else ps_p
                    wait("tensor", "sV", vev[g & 1])
                    if g == 0:
                        wait("tensor", "sA", PS.get("psm_scalar", 0))
                    for dc in range(8):
                        run("tensor", lambda e, a=lnslice(tpar, dc)[:, g * 128:(g + 1) * 128], b=wslice(par, 2, dc),
                            st=(dc == 0), bk=bank: e.matmul(bk[:, 0:128], a, b, start=st, stop=False))
                    tvmm = run("tensor", lambda e, bk=bank: e.matmul(bk[:, 0:128], ones[0:1, 0:128],
                                                                    bv_s[0:1, p * 128:(p + 1) * 128],
                                                                    start=False, stop=True), s="sP")
                    wait("vector", "sP", tvmm)
                    vev[g & 1] = run("vector", lambda e, o=vslot[:, g * 128:(g + 1) * 128], bk=bank:
                                     e.tensor_copy(o, bk[:, 0:128]), s="sV")

            def s_v0():
                U["vev"] = [PS["psm"], U["tkev"]]
                s_v(0)
            sec(s_v0)

            def s_v2():
                s_v(2)
                vready[p][tc] = max(U["vev"])
                PS["psm"] = U["vev"][0]
                PS["psp"] = U["vev"][1]
            sec(s_v2)

            def s_q():
                if tc >= 4:
                    return
                wait("tensor", "sV", PS["psp"])
                tqmm = None
                for dc in range(8):
                    tqmm = run("tensor", lambda e, a=wslice(par, 1, dc), b=lnslice(tpar, dc), st=(dc == 0), sp=(dc == 7):
                               e.matmul(ps_p[:], a, b, start=st, stop=sp), s="sP" if dc == 7 else None)
                wait("vector", "sP", tqmm)
                U["tqev"] = run("vector", lambda e, o=qslot: e.tensor_scalar(o, ps_p[:], bq32.ap()[:, p:p + 1],
                                                                            LOG2E / 8.0, ALU.add, ALU.mult), s="sV")
                PS["psp"] = U["tqev"]
            sec(s_q)

            jnkb = stage.ap()[:, 0:512]

            def s_norm(slot_name, base):
                slot = kslot if slot_name == "k" else qslot
                if slot is None:
                    return
                tok_ev = U["tkev"] if slot_name == "k" else U.get("tqev", 0)
                wait("vector", "sV", tok_ev)
                wait("vector", "sP", PS.get("normmm", 0))
                tsq = run("vector", lambda e, j=jnkb, i=slot: e.tensor_tensor(j, i, i, ALU.mult), s="sV")
                wait("vector", "sV", tsq)
                tsq = run("vector", lambda e: e.tensor_copy(m0buf.ap()[0:1, 15:16], m0buf.ap()[0:1, 14:15]), s="sV")
                for h in range(2):
                    wait("tensor", "sV", tsq)
                    wait("tensor", "sV", PS["psm"])
                    wait("tensor", "sA", PS.get("psm_scalar", 0))
                    tnq = run("tensor", lambda e, h=h: e.matmul(ps_m[0:1, :], hsel_s[:, h:h + 1], jnkb,
                                                                start=True, stop=True), s="sP")
                    PS["normmm"] = tnq
                    wait("vector", "sP", tnq)
                    runw("vector", lambda e: e.reduce_max(m0buf.ap()[0:1, 14:15], ps_m[0:1, :], axis=AX.X))
                    mslot = m0buf.ap()[0:1, par * 6 + base + h:par * 6 + base + h + 1]
                    PS["psm"] = runw("vector", lambda e, m=mslot: e.tensor_tensor(
                        m, m, m0buf.ap()[0:1, 14:15], ALU.max))
            sec(lambda: s_norm("k", 0))
            sec(lambda: s_norm("q", 2))

            def s_rope(slot_name):
                slot = kslot if slot_name == "k" else qslot
                if slot is None:
                    return
                cosap = cossin.ap()[:, tpar * 1024:tpar * 1024 + 512]
                sinap = cossin.ap()[:, tpar * 1024 + 512:tpar * 1024 + 1024]
                if tc == 0 and slot_name == "k":
                    wait("tensor", "sL", t_rot)
                wait("tensor", "sL", U["t_cs"])
                wait("tensor", "sV", PS["psm"])
                wait("tensor", "sA", PS.get("psm_scalar", 0))
                trot = run("tensor", lambda e, s_=slot: e.matmul(ps_m[:], rot_s[:], s_, start=True, stop=True), s="sP")
                j1 = stage.ap()[:, 512:1024]
                j2 = stage.ap()[:, 1024:1536]
                wait("vector", "sP", trot)
                run("vector", lambda e, j=j1, s_=slot, c=cosap: e.tensor_tensor(j, s_, c, ALU.mult))
                tm = run("vector", lambda e, j=j2, s_=sinap: e.tensor_tensor(j, ps_m[:], s_, ALU.mult), s="sV")
                PS["psm"] = tm
                tfin = run("vector", lambda e, o=slot, a=j1, b=j2: e.tensor_tensor(o, a, b, ALU.add), s="sV")
                if slot_name == "k":
                    kready[p][tc] = tfin
                else:
                    qrope[p] = max(qrope[p], tfin)
            sec(lambda: s_rope("k"))
            sec(lambda: s_rope("q"))

            def s_tail():
                lnt_free[tpar] = run("tensor", lambda e: e.matmul(ps_den[64:65, 0:1], ones[0:1, 0:1],
                                                                 ones[0:1, 0:1], start=True, stop=True), s="sP")
                cs_free[tpar] = max(kready[p][tc], qrope[p] if tc < 4 else 0)
                if tc == NTC - 1:
                    qready[p] = qrope[p]
                    wp_free[par] = lnt_free[tpar]
                    bp = par
                    runw("vector", lambda e: e.tensor_tensor(m0buf.ap()[0:1, 12:13],
                                                             m0buf.ap()[0:1, bp * 6 + 0:bp * 6 + 1],
                                                             m0buf.ap()[0:1, bp * 6 + 2:bp * 6 + 3], ALU.mult))
                    runw("vector", lambda e: e.tensor_tensor(m0buf.ap()[0:1, 13:14],
                                                             m0buf.ap()[0:1, bp * 6 + 1:bp * 6 + 2],
                                                             m0buf.ap()[0:1, bp * 6 + 3:bp * 6 + 4], ALU.mult))
                    runw("vector", lambda e: e.tensor_tensor(m0buf.ap()[0:1, 12:13], m0buf.ap()[0:1, 12:13],
                                                             m0buf.ap()[0:1, 13:14], ALU.max))
                    tz = run("vector", lambda e: e.memset(m0buf.ap()[0:1, bp * 6:bp * 6 + 4], 0.0), s="sV")
                    wait("scalar", "sV", tz)
                    _tl = run("scalar", lambda e: e.activation(m0buf.ap()[0:1, 13:14], m0buf.ap()[0:1, 12:13],
                                                               AF.Ln), s="sA")
                    wait("scalar", "sA", _tl)
                    tm0 = run("scalar", lambda e: e.activation(m0buf.ap()[0:1, 13:14], m0buf.ap()[0:1, 13:14],
                                                               AF.Exp, scale=0.5), s="sA")
                    wait("vector", "sA", tm0)
                    tmb = run("vector", lambda e, b=bp: e.tensor_copy(m0bf.ap()[0:1, b:b + 1],
                                                                      m0buf.ap()[0:1, 13:14]), s="sV")
                    wait("tensor", "sV", tmb)
                    wait("tensor", "sV", PS["psm"])
                    tmm = run("tensor", lambda e, b=bp: e.matmul(ps_m[:, 0:1], ones[0:1, 0:128],
                                                                 m0bf.ap()[0:1, b:b + 1], start=True, stop=True), s="sP")
                    wait("scalar", "sP", tmm)
                    wait("scalar", "sA", biasp_free[bp])
                    bready[p] = run("scalar", lambda e, b=bp: e.activation(biasp.ap()[:, b:b + 1], ps_m[:, 0:1],
                                                                          AF.Identity, scale=-LN2), s="sA")
                    PS["psm_scalar"] = bready[p]
            sec(s_tail)

        ik = [0]
        act_done = []
        av_done = []
        epi_tok = [0]

        pp_free = [0, 0, 0, 0]
        epi_act = [0]

        pre_scored = {}

        def emit_scores(p_, qb_, kt_, i):
            par_ = p_ & 1
            stX = ps_a if (i & 1) == 0 else ps_b
            tc = (kt_ * 128) // 512
            wait("tensor", "sV", kready[p_][tc])
            if qb_ == 0 and kt_ == 0:
                wait("tensor", "sV", qready[p_])
            if i >= 2:
                wait("tensor", "sA", act_done[i - 2])
            kofs = par_ * S + kt_ * 128
            qofs = par_ * TQ + qb_ * 512
            run("tensor", lambda e, st=stX, ko=kofs, qo=qofs:
                e.matmul(st[:, 0:512], big4.ap()[0:64, ko:ko + 128], qs.ap()[0:64, qo:qo + 512],
                         start=True, stop=True))
            tsc = run("tensor", lambda e, st=stX, ko=kofs, qo=qofs:
                      e.matmul(st[:, 512:1024], big4.ap()[64:128, ko:ko + 128],
                               qs.ap()[64:128, qo:qo + 512], start=True, stop=True), s="sP")
            return tsc, stX

        def emit_attention(p, units, pnext=None):
            par = p & 1
            for qb in range(NQB):
                exp_toks = [0] * NKT

                def emit_avden(kt):
                    pslot = pp.ap()[:, (kt & 3) * 1024:(kt & 3) * 1024 + 1024]
                    tc = (kt * 128) // 512
                    wait("tensor", "sA", exp_toks[kt])
                    wait("tensor", "sV", vready[p][tc])
                    if kt == 0:
                        wait("tensor", "sV", epi_tok[0])
                        wait("tensor", "sA", epi_act[0])
                    st0, sp0 = (kt == 0), (kt == NKT - 1)
                    vofs = par * S + kt * 128
                    run("tensor", lambda e, vo=vofs, pq=pslot, st=st0, sp=sp0:
                        e.matmul(ps_av[0:64, :], vs.ap()[:, vo:vo + 64], pq[:, 0:512], start=st, stop=sp,
                                 tile_position=(0, 0)))
                    run("tensor", lambda e, vo=vofs, pq=pslot, st=st0, sp=sp0:
                        e.matmul(ps_av[64:128, :], vs.ap()[:, vo + 64:vo + 128], pq[:, 512:1024], start=st, stop=sp,
                                 tile_position=(0, 64)))
                    run("tensor", lambda e, pq=pslot, st=st0, sp=sp0:
                        e.matmul(ps_den[0:1, :], ones[:, 0:1], pq[:, 0:512], start=st, stop=sp,
                                 tile_position=(0, 0)))
                    tav = run("tensor", lambda e, pq=pslot, st=st0, sp=sp0:
                              e.matmul(ps_den[32:33, :], ones[:, 0:1], pq[:, 512:1024], start=st, stop=sp,
                                       tile_position=(0, 32)), s="sP")
                    av_done.append(tav)
                    pp_free[kt & 3] = tav

                for kt in range(NKT):
                    i = ik[0]
                    ik[0] += 1
                    if (p, qb, kt) in pre_scored:
                        tsc, stX = pre_scored.pop((p, qb, kt))
                    else:
                        tsc, stX = emit_scores(p, qb, kt, i)
                    if kt % 2 == 0:
                        act_done.append(0)
                    else:
                        wait("scalar", "sP", tsc)
                        if qb == 0 and kt == 1:
                            wait("scalar", "sA", bready[p])
                        wait("scalar", "sP", pp_free[kt & 3])
                        base_ = ((kt - 1) & 3) * 1024
                        pslot2 = pp.ap()[:, base_:base_ + 2048]
                        texp = run("scalar", lambda e, o=pslot2, b=par:
                                   e.activation(o, ps_ab[:, 0:2048], AF.Exp, bias=biasp.ap()[:, b:b + 1], scale=LN2), s="sA")
                        act_done.append(texp)
                        act_done[i - 1] = texp
                        exp_toks[kt] = texp
                        exp_toks[kt - 1] = texp
                    if kt >= 2:
                        emit_avden(kt - 2)
                    if units is not None:
                        idx = qb * NKT + kt
                        want = min(len(units), (idx + 1) * len(units) // (NQB * NKT - 32))
                        while spliced[0] < want:
                            units[spliced[0]]()
                            spliced[0] += 1
                emit_avden(NKT - 2)
                emit_avden(NKT - 1)
                np_, nqb = (p, qb + 1) if qb + 1 < NQB else (pnext, 0)
                if np_ is not None:
                    for j in range(2):
                        pre_scored[(np_, nqb, j)] = emit_scores(np_, nqb, j, ik[0] + j)
                wait("scalar", "sP", av_done[-1])
                _t1 = run("scalar", lambda e: e.activation(r32.ap()[0:33, 0:512], ps_den[0:33, :], AF.Ln), s="sA")
                wait("scalar", "sA", _t1)
                tre = run("scalar", lambda e: e.activation(r32.ap()[0:33, 0:512], r32.ap()[0:33, 0:512],
                                                           AF.Exp, scale=-1.0), s="sA")
                epi_act[0] = tre
                wait("vector", "sP", av_done[-1])
                tcv = run("vector", lambda e: e.tensor_copy(av32[:], ps_av[:]), s="sV")
                rb1 = stage.ap()[0:1, 2048:2560]
                rb2 = stage.ap()[32:33, 2048:2560]
                wait("vector", "sA", tre)
                trc = runw("vector", lambda e: e.tensor_copy(stage.ap()[0:33, 2048:2560], r32.ap()[0:33, 0:512]))
                wait("tensor", "sV", trc)
                run("tensor", lambda e: e.matmul(ps_av[0:64, :], ones[0:1, 0:64], rb1, start=True, stop=True,
                                                 tile_position=(0, 0)))
                tbc = run("tensor", lambda e: e.matmul(ps_av[64:128, :], ones[32:33, 0:64], rb2,
                                                       start=True, stop=True, tile_position=(32, 64)), s="sP")
                wait("vector", "sP", tbc)
                aslot = attnT.ap()[:, p * TQ + qb * 512:p * TQ + (qb + 1) * 512]
                epi_tok[0] = run("vector", lambda e, o=aslot: e.tensor_tensor(o, av32[:], ps_av[:], ALU.mult), s="sV")
            ks_free[par] = av_done[-1]
            vs_free[par] = av_done[-1]
            qs_free[par] = av_done[-1]
            biasp_free[par] = act_done[-1]

        spliced = [0]
        if "pairs" in DBG_PHASES:
            for tc in range(NTC if DBG_NPAIR >= 0 else -DBG_NPAIR):
                emit_proj_unit(0, tc)
            for p in range(max(DBG_NPAIR, 0)):
                if p + 1 < DBG_NPAIR:
                    units = []
                    for tc in range(NTC):
                        emit_proj_unit(p + 1, tc, bites=units)
                else:
                    units = None
                spliced[0] = 0
                emit_attention(p, units, pnext=p + 1 if p + 1 < DBG_NPAIR else None)

        barrier()
        if DBG_DUMP:
            dd = 0
            dd = dma("gpsimd", dmp_ln[:, :], ln1T_d[:, :], s="sG")
            dd = dma("gpsimd", dmp_ks[:, :], big4.ap()[:, :], s="sG")
            dd = dma("gpsimd", dmp_vs[:, :], vs.ap()[:, :], s="sG")
            dd = dma("gpsimd", dmp_qs[:, :], qs.ap()[:, :], s="sG")
            dd = dma("gpsimd", dmp_at[:, :], attnT.ap()[:, :], s="sG")
            dd = dma("gpsimd", dmp_bp[:, :], biasp.ap()[:, :], s="sG")
            dd = dma("gpsimd", dmp_m0[:, :], m0buf.ap()[:, :], s="sG")
            dd = dma("gpsimd", dmp_st[:, :], sttrace.ap()[:, :], s="sG")
            for hl in range(2):
                for dc in range(8):
                    dd = dma("gpsimd", dmp_wc[:, (hl * 8 + dc) * 1024:(hl * 8 + dc + 1) * 1024],
                             wcat_d[hl, dc * 128:(dc + 1) * 128, :], s="sG")
            wait("gpsimd", "sG", dd)
            barrier()

        if "ph3" not in DBG_PHASES:
            tz0 = run("vector", lambda e: e.memset(outb[:], 0.0), s="sV")
            wait("gpsimd", "sV", tz0)
            td0 = 0
            for t in range(NT3):
                td0 = dma("gpsimd", out_d[t * 128:(t + 1) * 128, :], outb.ap()[:, 0:1024], s="sG")
            wait("gpsimd", "sG", td0)
            wait("gpsimd", "sG", t_wcd)
            _finish(block, ops)
            return nc
        t_wo = 0
        for dc in range(8):
            t_wo = dma("sync", warena.ap()[:, dc * 1024:(dc + 1) * 1024], woT[dc * 128:(dc + 1) * 128, :])
        t_wcl = 0
        for hl in range(2):
            for dc in range(8):
                t_wcl = dma("sync", big4.ap()[:, hl * 8192 + dc * 1024:hl * 8192 + (dc + 1) * 1024],
                            wcat_d[hl, dc * 128:(dc + 1) * 128, :])
        p3_x0 = [0, 0]
        p3_adone = [0, 0]
        p3_v3free = [0, 0]
        p3_w3free = [0, 0]
        p3_psa = [0, 0]
        p3_psb = [0]
        p3_tv2 = [0, 0]
        p3_tF = [0, 0]
        tr3_cp = [0, 0]
        tr3_last = [0]
        outdma = [0, 0]
        p3_sqscr = pp.ap()[:, 0:1024]
        p3_jnkb = pp.ap()[:, 1024:2048]

        def p3_x1(t):
            return vs.ap()[:, t * 1024:(t + 1) * 1024]

        def p3_slot(par, j):
            return stats.ap()[:, par * 16 + j:par * 16 + j + 1]

        p3_tO = [0, 0]

        def p3A1(t):
            par = t & 1
            x0a = stage.ap()[:, par * 1024:(par + 1) * 1024]
            wait("gpsimd", "sV", p3_adone[par])
            p3_x0[par] = dma("gpsimd", x0a, x_in[t * 128:(t + 1) * 128, :], s="sG")
            if t == 0:
                wait("tensor", "sL", t_wo)
            psout = (ps_a[:, 0:512], ps_a[:, 512:1024]) if par == 0 \
                else (ps_av.ap()[:, :], ps_den.ap()[:, :])
            wait("tensor", "sA", p3_psa[par])
            for ns in range(2):
                for fc in range(8):
                    run("tensor", lambda e, fc=fc, ns=ns, t=t:
                        e.matmul(psout[ns],
                                 attnT.ap()[:, fc * TQ + t * 128:fc * TQ + (t + 1) * 128],
                                 warena.ap()[:, fc * 1024 + ns * 512:fc * 1024 + (ns + 1) * 512],
                                 start=(fc == 0), stop=False))
                p3_tO[par] = run("tensor", lambda e, ns=ns: e.matmul(psout[ns], ones[0:1, 0:128],
                                                                    bo_s[0:1, ns * 512:(ns + 1) * 512], start=False, stop=True),
                                 s="sP" if ns == 1 else None)

        def p3A2(t):
            par = t & 1
            x0a = stage.ap()[:, par * 1024:(par + 1) * 1024]
            x1a = p3_x1(t)
            v3 = v32.ap()[:, par * 1024:(par + 1) * 1024]
            sm, sq, mn, vr = (p3_slot(par, j) for j in range(4))
            rs, nm = p3_slot(par, 4), p3_slot(par, 5)
            wait("scalar", "sP", p3_tO[par])
            wait("scalar", "sV", p3_v3free[par])
            if par == 0:
                oc = run("scalar", lambda e, o=v3, s_=sm: e.activation(o, ps_a[:], AF.Copy, accum_out=s_), s="sA")
            else:
                run("scalar", lambda e, o=v3[:, 0:512], s_=sm: e.activation(o, ps_av[:], AF.Copy, accum_out=s_), s="sA")
                oc = run("scalar", lambda e, o=v3[:, 512:1024], s_=m0buf.ap()[:, par:par + 1]:
                         e.activation(o, ps_den[:], AF.Copy, accum_out=s_), s="sA")
            p3_psa[par] = oc
            wait("vector", "sA", oc)
            _ts = run("vector", lambda e, i=v3, q=sq: e.scalar_tensor_tensor(p3_sqscr, i, 1.0, i, ALU.mult, ALU.mult, accum_out=q), s="sV")
            wait("vector", "sV", _ts)
            if par == 1:
                runw("vector", lambda e, s_=sm: e.tensor_tensor(s_, s_, m0buf.ap()[:, par:par + 1], ALU.add))
            runw("vector", lambda e, m=mn, s_=sm: e.tensor_scalar_mul(m, s_, 1.0 / 1024.0))
            runw("vector", lambda e, v=vr, q=sq: e.tensor_scalar_mul(v, q, 1.0 / 1024.0))
            runw("vector", lambda e, m=mn: e.tensor_tensor(r32.ap()[:, 0:1], m, m, ALU.mult))
            tv = runw("vector", lambda e, v=vr: e.tensor_tensor(v, v, r32.ap()[:, 0:1], ALU.subtract))
            wait("scalar", "sV", tv)
            _tl = run("scalar", lambda e, o=rs, v=vr: e.activation(o, v, AF.Ln, bias=eps_t.ap()[:, 0:1]), s="sA")
            wait("scalar", "sA", _tl)
            ta = run("scalar", lambda e, o=rs: e.activation(o, o, AF.Exp, scale=-0.5), s="sA")
            wait("scalar", "sA", ta)
            wait("vector", "sA", ta)
            runw("vector", lambda e, n=nm, m=mn, r=rs: e.tensor_scalar(n, m, r, -1.0, ALU.mult, ALU.mult))
            wait("vector", "sG", p3_x0[par])
            tvf = runw("vector", lambda e, i=v3, r=rs, n=nm: e.tensor_scalar(p3_jnkb, i, r, n, ALU.mult, ALU.add))
            p3_v3free[par] = tvf
            tx1 = run("vector", lambda e, o=x1a, x0=x0a: e.tensor_tensor(o, p3_jnkb, x0, ALU.add), s="sV")
            p3_adone[par] = tx1
            sm2, sq2, mn2, vr2 = (p3_slot(par, 6 + j) for j in range(4))
            wait("vector", "sV", tx1)
            run("vector", lambda e, o=sm2, i=x1a: e.reduce_sum(o, i, axis=AX.X))
            _ts = run("vector", lambda e, i=x1a, q=sq2: e.scalar_tensor_tensor(p3_sqscr, i, 1.0, i, ALU.mult, ALU.mult, accum_out=q), s="sV")
            wait("vector", "sV", _ts)
            runw("vector", lambda e, m=mn2, s_=sm2: e.tensor_scalar_mul(m, s_, 1.0 / 1024.0))
            runw("vector", lambda e, v=vr2, q=sq2: e.tensor_scalar_mul(v, q, 1.0 / 1024.0))
            runw("vector", lambda e, m=mn2: e.tensor_tensor(r32.ap()[:, 0:1], m, m, ALU.mult))
            p3_tv2[par] = runw("vector", lambda e, v=vr2: e.tensor_tensor(v, v, r32.ap()[:, 0:1], ALU.subtract))

        def p3B(t):
            par = t & 1
            x1a = p3_x1(t)
            ln2a = stage.ap()[:, 2048 + par * 1024:2048 + (par + 1) * 1024]
            ln2t = lnt.ap()[:, 0:1024]
            mn2, vr2 = p3_slot(par, 8), p3_slot(par, 9)
            rs2, nm2 = p3_slot(par, 10), p3_slot(par, 11)
            wait("scalar", "sV", p3_tv2[par])
            _tl = run("scalar", lambda e, o=rs2, v=vr2: e.activation(o, v, AF.Ln, bias=eps_t.ap()[:, 0:1]), s="sA")
            wait("scalar", "sA", _tl)
            ta2 = run("scalar", lambda e, o=rs2: e.activation(o, o, AF.Exp, scale=-0.5), s="sA")
            wait("scalar", "sA", ta2)
            wait("vector", "sA", ta2)
            tn2 = runw("vector", lambda e, n=nm2, m=mn2, r=rs2: e.tensor_scalar(n, m, r, -1.0, ALU.mult, ALU.mult))
            wait("scalar", "sV", tn2)
            wait("scalar", "sP", tr3_last[0])
            tl2 = run("scalar", lambda e, o=ln2a, i=x1a, r=rs2, n=nm2: e.activation(o, i, AF.Identity, bias=n, scale=r), s="sA")
            trbank = (ps_p.ap().bitcast(BF16)[:, 0:128], ps_m.ap().bitcast(BF16)[:, 0:128])
            wait("tensor", "sA", tl2)
            wait("scalar", "sP", p3_tF[1 - par])
            wait("vector", "sP", p3_tF[1 - par])
            for dc in range(8):
                b = dc & 1
                wait("tensor", "sA" if b == 0 else "sV", tr3_cp[b])
                ttr = run("tensor", lambda e, o=trbank[b], i=ln2a[:, dc * 128:(dc + 1) * 128]:
                          e.transpose(o, i, ident_s[:]), s="sP")
                if b == 0:
                    wait("scalar", "sP", ttr)
                    tr3_cp[0] = run("scalar", lambda e, o=ln2t[:, dc * 128:(dc + 1) * 128], i=trbank[0]:
                                    e.copy(o, i), s="sA")
                else:
                    wait("vector", "sP", ttr)
                    tr3_cp[1] = run("vector", lambda e, o=ln2t[:, dc * 128:(dc + 1) * 128], i=trbank[1]:
                                    e.tensor_copy(o, i), s="sV")
                tr3_last[0] = ttr
            wait("tensor", "sA", tr3_cp[0])
            wait("tensor", "sV", tr3_cp[1])
            if t == 0:
                wait("tensor", "sL", t_wcl)
                wait("tensor", "sV", t_fbl)
            wait("tensor", "sA", p3_psb[0])
            for ns in range(2):
                for hl in range(2):
                    for dc in range(8):
                        run("tensor", lambda e, dc=dc, hl=hl, ns=ns:
                            e.matmul(ps_b[:, ns * 512:(ns + 1) * 512],
                                     ln2t[:, dc * 128:(dc + 1) * 128],
                                     big4.ap()[:, hl * 8192 + dc * 1024 + ns * 512:hl * 8192 + dc * 1024 + (ns + 1) * 512],
                                     start=(hl == 0 and dc == 0), stop=False))
                run("tensor", lambda e, ns=ns: e.matmul(ps_b[:, ns * 512:(ns + 1) * 512], ones[0:1, 0:128],
                                                        fbh[0:1, ns * 512:(ns + 1) * 512], start=False, stop=False))
                tF = run("tensor", lambda e, ns=ns: e.matmul(ps_b[:, ns * 512:(ns + 1) * 512], ones[0:1, 0:128],
                                                             fbl[0:1, ns * 512:(ns + 1) * 512], start=False, stop=True),
                         s="sP" if ns == 1 else None)
            p3_tF[par] = tF

        def p3C(t):
            par = t & 1
            x1a = p3_x1(t)
            w3 = qs.ap().bitcast(F32)[:, par * 1024:(par + 1) * 1024]
            sm3, sq3, mn3, vr3 = (p3_slot(par, 12 + j) for j in range(4))
            rs, nm = p3_slot(par, 4), p3_slot(par, 5)
            wait("scalar", "sP", p3_tF[par])
            wait("scalar", "sV", p3_w3free[par])
            tw3 = run("scalar", lambda e, o=w3, s_=sm3: e.activation(o, ps_b[:], AF.Copy, accum_out=s_), s="sA")
            p3_psb[0] = tw3
            wait("vector", "sA", tw3)
            _ts = run("vector", lambda e, i=w3, q=sq3: e.scalar_tensor_tensor(p3_sqscr, i, 1.0, i, ALU.mult, ALU.mult, accum_out=q), s="sV")
            wait("vector", "sV", _ts)
            runw("vector", lambda e, m=mn3, s_=sm3: e.tensor_scalar_mul(m, s_, 1.0 / 1024.0))
            runw("vector", lambda e, v=vr3, q=sq3: e.tensor_scalar_mul(v, q, 1.0 / 1024.0))
            runw("vector", lambda e, m=mn3: e.tensor_tensor(r32.ap()[:, 0:1], m, m, ALU.mult))
            tv3 = runw("vector", lambda e, v=vr3: e.tensor_tensor(v, v, r32.ap()[:, 0:1], ALU.subtract))
            wait("scalar", "sV", tv3)
            _tl = run("scalar", lambda e, o=rs, v=vr3: e.activation(o, v, AF.Ln, bias=eps_t.ap()[:, 0:1]), s="sA")
            wait("scalar", "sA", _tl)
            ta3 = run("scalar", lambda e, o=rs: e.activation(o, o, AF.Exp, scale=-0.5), s="sA")
            wait("scalar", "sA", ta3)
            wait("vector", "sA", ta3)
            runw("vector", lambda e, n=nm, m=mn3, r=rs: e.tensor_scalar(n, m, r, -1.0, ALU.mult, ALU.mult))
            outa = outb.ap()[:, par * 1024:(par + 1) * 1024]
            wait("vector", "sG", outdma[par])
            runw("vector", lambda e, j=w3, r=rs, n=nm: e.tensor_scalar(j, j, r, n, ALU.mult, ALU.add))
            tout = run("vector", lambda e, o=outa, j=w3, x1=x1a: e.tensor_tensor(o, j, x1, ALU.add), s="sV")
            p3_w3free[par] = tout
            wait("gpsimd", "sV", tout)
            outdma[par] = dma("gpsimd", out_d[t * 128:(t + 1) * 128, :], outa, s="sG")

        for step in range(NT3 + 2):
            if step < NT3:
                p3A1(step)
            if 0 <= step - 2 < NT3:
                p3C(step - 2)
            if 0 <= step - 1 < NT3:
                p3B(step - 1)
            if step < NT3:
                p3A2(step)

        wait("gpsimd", "sG", outdma[0])
        wait("gpsimd", "sG", outdma[1])
        wait("gpsimd", "sG", t_wcd)
        _finish(block, ops)
    return nc


def _finish(block, ops):
    @block.sync
    def _(e):
        for fn in ops["sync"]:
            fn(e)

    @block.gpsimd
    def _(e):
        for fn in ops["gpsimd"]:
            fn(e)

    @block.tensor
    def _(e):
        for fn in ops["tensor"]:
            fn(e)

    @block.scalar
    def _(e):
        for fn in ops["scalar"]:
            fn(e)

    @block.vector
    def _(e):
        for fn in ops["vector"]:
            fn(e)


def kernel(L0_attn_input_embeds, L0_attn_Wq, L0_attn_bq, L0_attn_Wk, L0_attn_bk,
           L0_attn_Wv, L0_attn_bv, L0_attn_Wo, L0_attn_bo, L0_ffn_G1_R, L0_ffn_G2_R,
           L0_ffn_G1_I, L0_ffn_G2_I, L0_ffn_bias_real, L0_ffn_bias_imag):
    x = np.asarray(L0_attn_input_embeds)
    wq = np.asarray(L0_attn_Wq); wk = np.asarray(L0_attn_Wk)
    wv = np.asarray(L0_attn_Wv); wo = np.asarray(L0_attn_Wo)
    bq = np.asarray(L0_attn_bq); bk = np.asarray(L0_attn_bk)
    bv = np.asarray(L0_attn_bv); bo = np.asarray(L0_attn_bo)
    g1r_ = np.asarray(L0_ffn_G1_R, dtype=np.float32)
    g2r_ = np.asarray(L0_ffn_G2_R, dtype=np.float32)
    g1i_ = np.asarray(L0_ffn_G1_I, dtype=np.float32)
    g2i_ = np.asarray(L0_ffn_G2_I, dtype=np.float32)
    br = np.asarray(L0_ffn_bias_real, dtype=np.float32)
    bi = np.asarray(L0_ffn_bias_imag, dtype=np.float32)

    shared = {
        "wqT": np.ascontiguousarray(wq.T), "wkT": np.ascontiguousarray(wk.T),
        "wvT": np.ascontiguousarray(wv.T), "woT": np.ascontiguousarray(wo.T),
        "bqT": np.ascontiguousarray(bq.reshape(8, 128).T),
        "bkT": np.ascontiguousarray(bk.reshape(8, 128).T),
        "bv": np.ascontiguousarray(bv.reshape(1, D)),
        "bo": np.ascontiguousarray(bo.reshape(1, D)),
        "fbias": np.ascontiguousarray(np.concatenate([br, bi]).reshape(1, D)),
        "g1r": np.ascontiguousarray(np.transpose(g1r_[0], (2, 1, 0))),
        "g1i": np.ascontiguousarray(np.transpose(g1i_[0], (2, 1, 0))),
        "g2r": np.ascontiguousarray(np.transpose(g2r_[:, :, 0, :], (2, 0, 1))),
        "g2i": np.ascontiguousarray(np.transpose(g2i_[:, :, 0, :], (2, 0, 1))),
        "identT": np.eye(128, dtype=bf16),
    }
    hs = np.zeros((128, 33), dtype=bf16)
    hs[0:64, 0] = 1.0
    hs[64:128, 1] = 1.0
    shared["hsel"] = hs
    R = np.zeros((128, 128), dtype=np.float32)
    for blk in range(2):
        o = blk * 64
        for d in range(32):
            R[o + d, o + d + 32] = -1.0
            R[o + d + 32, o + d] = 1.0
    shared["rotT"] = np.ascontiguousarray(R.T).astype(bf16)
    inv = (1.0 / (10000.0 ** (np.arange(0, 64, 2, dtype=np.float32) / 64.0))).astype(np.float32)
    tpos = np.arange(S, dtype=np.float32)
    fr = np.outer(inv, tpos)
    cos128 = np.tile(np.cos(fr).astype(bf16), (4, 1))
    sin128 = np.tile(np.sin(fr).astype(bf16), (4, 1))

    nc = build_program()
    in_maps = []
    for c in range(NCORE):
        beta, qt = c // 4, c % 4
        roll = qt * TQ
        m = dict(shared)
        m["x"] = np.ascontiguousarray(np.roll(x[beta], -roll, axis=0))
        m["cosT"] = np.ascontiguousarray(np.roll(cos128, -roll, axis=1))
        m["sinT"] = np.ascontiguousarray(np.roll(sin128, -roll, axis=1))
        in_maps.append(m)

    res = run_bass_kernel_spmd(nc, in_maps, core_ids=list(range(NCORE)))
    out = np.empty((B, S, D), dtype=np.float32)
    for c in range(NCORE):
        beta, qt = c // 4, c % 4
        out[beta, qt * TQ:(qt + 1) * TQ, :] = res.results[c]["out"]
    return out


# version 21
# speedup vs baseline: 1.4581x; 1.4581x over previous
import numpy as np
import ml_dtypes
import concourse.bass as bass
import concourse.mybir as mybir
from concourse.bass_utils import run_bass_kernel_spmd

F32 = mybir.dt.float32
BF16 = mybir.dt.bfloat16
AF = mybir.ActivationFunctionType
ALU = mybir.AluOpType
AX = mybir.AxisListType
bf16 = ml_dtypes.bfloat16

B = 2
S = 8192
D = 1024
NCORE = 8
TQ = 2048
NPAIR = 8
NKT = S // 128
NQB = TQ // 512
NTC = S // 512
NT1 = S // 128
NT3 = TQ // 128
LOG2E = float(np.log2(np.e))
LN2 = float(np.log(2.0))


DBG_PHASES = ("ph0", "ph1", "pairs", "ph3")
DBG_NPAIR = NPAIR
DBG_PROJ = ("k", "v", "q", "norms", "rope", "m0")
DBG_DUMP = False
DUMMY_N = 0
DBG_PH1_BARRIER = False
DBG_PH1_GUARD = False
DBG_PH1_SELFSYNC = False


def build_program():
    nc = bass.Bass("TRN2", target_bir_lowering=False)

    x_in = nc.dram_tensor("x", [S, D], BF16, kind="ExternalInput").ap()
    wqT = nc.dram_tensor("wqT", [D, D], BF16, kind="ExternalInput").ap()
    wkT = nc.dram_tensor("wkT", [D, D], BF16, kind="ExternalInput").ap()
    wvT = nc.dram_tensor("wvT", [D, D], BF16, kind="ExternalInput").ap()
    woT = nc.dram_tensor("woT", [D, D], BF16, kind="ExternalInput").ap()
    bqT = nc.dram_tensor("bqT", [128, 8], BF16, kind="ExternalInput").ap()
    bkT = nc.dram_tensor("bkT", [128, 8], BF16, kind="ExternalInput").ap()
    bv_r = nc.dram_tensor("bv", [1, D], BF16, kind="ExternalInput").ap()
    bo_r = nc.dram_tensor("bo", [1, D], BF16, kind="ExternalInput").ap()
    fbias = nc.dram_tensor("fbias", [1, D], F32, kind="ExternalInput").ap()
    g1r = nc.dram_tensor("g1r", [32, 4, 32], F32, kind="ExternalInput").ap()
    g1i = nc.dram_tensor("g1i", [32, 4, 32], F32, kind="ExternalInput").ap()
    g2r = nc.dram_tensor("g2r", [16, 4, 16], F32, kind="ExternalInput").ap()
    g2i = nc.dram_tensor("g2i", [16, 4, 16], F32, kind="ExternalInput").ap()
    cosT = nc.dram_tensor("cosT", [128, S], BF16, kind="ExternalInput").ap()
    sinT = nc.dram_tensor("sinT", [128, S], BF16, kind="ExternalInput").ap()
    identT = nc.dram_tensor("identT", [128, 128], BF16, kind="ExternalInput").ap()
    rotT = nc.dram_tensor("rotT", [128, 128], BF16, kind="ExternalInput").ap()
    hsel = nc.dram_tensor("hsel", [128, 33], BF16, kind="ExternalInput").ap()
    out_d = nc.dram_tensor("out", [TQ, D], F32, kind="ExternalOutput").ap()
    if DBG_DUMP:
        dmp_ln = nc.dram_tensor("dmp_ln", [D, S], BF16, kind="ExternalOutput").ap()
        dmp_ks = nc.dram_tensor("dmp_ks", [128, 2 * S], BF16, kind="ExternalOutput").ap()
        dmp_vs = nc.dram_tensor("dmp_vs", [128, 2 * S], BF16, kind="ExternalOutput").ap()
        dmp_qs = nc.dram_tensor("dmp_qs", [128, 2 * TQ], BF16, kind="ExternalOutput").ap()
        dmp_at = nc.dram_tensor("dmp_at", [128, NPAIR * TQ], BF16, kind="ExternalOutput").ap()
        dmp_bp = nc.dram_tensor("dmp_bp", [128, 2], F32, kind="ExternalOutput").ap()
        dmp_wc = nc.dram_tensor("dmp_wc", [128, 2 * S], BF16, kind="ExternalOutput").ap()
        dmp_m0 = nc.dram_tensor("dmp_m0", [128, 16], F32, kind="ExternalOutput").ap()
        dmp_st = nc.dram_tensor("dmp_st", [128, 512], F32, kind="ExternalOutput").ap()
        dmp_ln2 = nc.dram_tensor("dmp_ln2", [D, S], BF16, kind="ExternalOutput").ap()
    ln1T_d = nc.dram_tensor("ln1T_d", [D, S], BF16)
    wcat_d = nc.dram_tensor("wcat_d", [2, D, D], BF16)

    from contextlib import ExitStack
    with ExitStack() as _st:
        ec = _st.enter_context
        attnT = ec(nc.sbuf_tensor("attnT", [128, NPAIR * TQ], BF16))
        big4 = ec(nc.sbuf_tensor("big4", [128, 2 * S], BF16))
        vs = ec(nc.sbuf_tensor("vs", [128, 2 * S], BF16))
        lnt = ec(nc.sbuf_tensor("lnt", [128, 2 * 4096], BF16))
        qs = ec(nc.sbuf_tensor("qs", [128, 2 * TQ], BF16))
        pp = ec(nc.sbuf_tensor("pp", [128, 4 * 1024], BF16))
        cossin = ec(nc.sbuf_tensor("cossin", [128, 2 * 1024], BF16))
        wpair = ec(nc.sbuf_tensor("wpair", [128, 2 * 3072], BF16))
        warena = ec(nc.sbuf_tensor("warena", [128, 8192], BF16))
        stage = ec(nc.sbuf_tensor("stage", [128, 4096], BF16))
        av32 = ec(nc.sbuf_tensor("av32", [128, 512], F32))
        v32 = ec(nc.sbuf_tensor("v32", [128, 2048], F32))
        outb = ec(nc.sbuf_tensor("outb", [128, 2048], F32))
        r32 = ec(nc.sbuf_tensor("r32", [128, 512], F32))
        ones = ec(nc.sbuf_tensor("ones", [128, 128], BF16))
        ident_s = ec(nc.sbuf_tensor("ident_s", [128, 128], BF16))
        rot_s = ec(nc.sbuf_tensor("rot_s", [128, 128], BF16))
        hsel_s = ec(nc.sbuf_tensor("hsel_s", [128, 33], BF16))
        bq32 = ec(nc.sbuf_tensor("bq32", [128, 8], F32))
        bk32 = ec(nc.sbuf_tensor("bk32", [128, 8], F32))
        bqbk = ec(nc.sbuf_tensor("bqbk", [128, 16], BF16))
        bv_s = ec(nc.sbuf_tensor("bv_s", [1, D], BF16))
        bo_s = ec(nc.sbuf_tensor("bo_s", [1, D], BF16))
        fbh = ec(nc.sbuf_tensor("fbh", [1, D], BF16))
        fbl = ec(nc.sbuf_tensor("fbl", [1, D], BF16))
        fb32 = ec(nc.sbuf_tensor("fb32", [1, D], F32))
        eps_t = ec(nc.sbuf_tensor("eps_t", [128, 1], F32))
        stats = ec(nc.sbuf_tensor("stats", [128, 64], F32))
        m0buf = ec(nc.sbuf_tensor("m0buf", [128, 16], F32))
        m0bf = ec(nc.sbuf_tensor("m0bf", [1, 2], BF16))
        biasp = ec(nc.sbuf_tensor("biasp", [128, 2], F32))
        astore = ec(nc.sbuf_tensor("astore", [128, 1024], F32))
        bstore = ec(nc.sbuf_tensor("bstore", [128, 128], F32))
        sttrace = ec(nc.sbuf_tensor("sttrace", [128, 512], F32))
        ps_ab = ec(nc.psum_tensor("ps_ab", [128, 2048], F32))
        ps_a = ps_ab.ap()[:, 0:1024]
        ps_b = ps_ab.ap()[:, 1024:2048]
        ps_av = ec(nc.psum_tensor("ps_av", [128, 512], F32))
        ps_den = ec(nc.psum_tensor("ps_den", [128, 512], F32))
        ps_p = ec(nc.psum_tensor("ps_p", [128, 512], F32))
        ps_m = ec(nc.psum_tensor("ps_m", [128, 512], F32))
        sL = ec(nc.semaphore("sL"))
        sG = ec(nc.semaphore("sG"))
        sP = ec(nc.semaphore("sP"))
        sA = ec(nc.semaphore("sA"))
        sV = ec(nc.semaphore("sV"))
        sB = ec(nc.semaphore("sB"))
        block = ec(nc.Block())
        ENG = ("sync", "gpsimd", "tensor", "scalar", "vector")
        ops = {e: [] for e in ENG}
        cnt = {"sL": 0, "sG": 0, "sP": 0, "sA": 0, "sV": 0}
        sem = {"sL": sL, "sG": sG, "sP": sP, "sA": sA, "sV": sV}

        def emit(eng, fn):
            ops[eng].append(fn)

        def dma(eng, out, in_, s="sL"):
            cnt[s] += 16
            tok = cnt[s]
            sm = sem[s]
            emit(eng, lambda e, o=out, i=in_, m=sm: e.dma_start(out=o, in_=i).then_inc(m, 16))
            return tok

        def wait(eng, s, tok):
            if tok and tok > 0:
                sm = sem[s]
                emit(eng, lambda e, m=sm, t=tok: e.wait_ge(m, t))

        def runw(eng, fn, s="sV"):
            tok = run(eng, fn, s=s)
            wait(eng, s, tok)
            return tok

        def run(eng, fn, s=None):
            if s is None:
                emit(eng, fn)
                return None
            cnt[s] += 1
            tok = cnt[s]
            sm = sem[s]
            emit(eng, lambda e, f=fn, m=sm: f(e).then_inc(m, 1))
            return tok

        nbar = [0]

        def barrier():
            nbar[0] += 1
            target = 5 * nbar[0]
            emit("sync", lambda e, t=cnt["sL"]: e.wait_ge(sL, t) if t else None)
            emit("gpsimd", lambda e, t=cnt["sG"]: e.wait_ge(sG, t) if t else None)
            for eng in ENG:
                emit(eng, lambda e: e.sem_inc(sB, 1))
                emit(eng, lambda e, t=target: e.wait_ge(sB, t))

        t_id = dma("sync", ident_s[:], identT)
        t_rot = dma("sync", rot_s[:], rotT)
        t_rot = dma("sync", hsel_s[:], hsel)
        dma("sync", bqbk[:, 0:8], bqT)
        t_bq = dma("sync", bqbk[:, 8:16], bkT)
        dma("sync", bv_s[0:1, :], bv_r)
        dma("sync", bo_s[0:1, :], bo_r)
        t_fb = dma("sync", fb32[:], fbias)
        t_ab = 0
        for ri, g1 in ((0, g1r), (1, g1i)):
            for ch in range(4):
                for g in range(16):
                    dst = astore.ap()[g:128:16,
                                      (ri * 4 + ch) * 128:(ri * 4 + ch) * 128 + 128] \
                        .rearrange("p (c b) -> p c b", b=32)
                    t_ab = dma("sync", dst, g1[8 * ch:8 * ch + 8, :, :])
        for ri, g2 in ((0, g2r), (1, g2i)):
            for d8 in range(8):
                dst = bstore.ap()[16 * d8:16 * d8 + 16, ri * 64:(ri + 1) * 64] \
                    .rearrange("p (c e) -> p c e", e=16)
                t_ab = dma("sync", dst, g2[:, :, :])
        run("vector", lambda e: e.memset(ones[:], 1.0))
        run("vector", lambda e: e.memset(eps_t[:], 1e-5))
        run("vector", lambda e: e.memset(m0buf[:], 0.0))
        wait("vector", "sL", t_bq)
        run("vector", lambda e: e.tensor_copy(bq32[:], bqbk[:, 0:8]))
        run("vector", lambda e: e.tensor_copy(bk32[:], bqbk[:, 8:16]))
        wait("vector", "sL", t_fb)
        run("vector", lambda e: e.tensor_copy(fbh[0:1, :], fb32[:]))
        t_fbl = run("vector", lambda e: e.tensor_tensor(fbl[0:1, :], fb32[:], fbh[0:1, :], ALU.subtract), s="sV")

        xw = warena.ap().bitcast(F32)
        wait("vector", "sL", t_ab)
        for ri in range(2):
            for ch in range(4):
                xt = xw[:, (ri * 4 + ch) * 512:(ri * 4 + ch) * 512 + 512] \
                    .rearrange("p (b e) -> p b e", e=16)
                for c in range(4):
                    a_v = astore.ap()[:, (ri * 4 + ch) * 128 + c * 32:(ri * 4 + ch) * 128 + (c + 1) * 32]
                    a_b = a_v[:, :, None].broadcast_to([128, 32, 16])
                    b_v = bstore.ap()[:, ri * 64 + c * 16:ri * 64 + (c + 1) * 16]
                    b_b = b_v[:, None, :].broadcast_to([128, 32, 16])
                    if c == 0:
                        run("vector", lambda e, o=xt, a=a_b, b=b_b: e.tensor_tensor(o, a, b, ALU.mult))
                    else:
                        jt = v32.ap()[:, 0:512].rearrange("p (b e) -> p b e", e=16)
                        run("vector", lambda e, o=jt, a=a_b, b=b_b: e.tensor_tensor(o, a, b, ALU.mult))
                        run("vector", lambda e, o=xt, j=jt: e.tensor_tensor(o, o, j, ALU.add))
        t_wc = 0
        for dc in range(8):
            for half in range(2):
                ch = dc % 4
                if dc < 4:
                    ri, neg = (0, False) if half == 0 else (1, False)
                else:
                    ri, neg = (1, True) if half == 0 else (0, False)
                src = xw[:, (ri * 4 + ch) * 512:(ri * 4 + ch) * 512 + 512]
                hi = big4.ap()[:, dc * 1024 + half * 512:dc * 1024 + half * 512 + 512]
                lo = big4.ap()[:, 8192 + dc * 1024 + half * 512:8192 + dc * 1024 + half * 512 + 512]
                if neg:
                    run("vector", lambda e, s=src: e.tensor_scalar_mul(s, s, -1.0))
                run("vector", lambda e, h=hi, s=src: e.tensor_copy(h, s))
                t_wc = run("vector", lambda e, l=lo, s=src, h=hi: e.tensor_tensor(l, s, h, ALU.subtract), s="sV")
        wait("gpsimd", "sV", t_wc)
        t_wcd = 0
        for hl in range(2):
            for dc in range(8):
                t_wcd = dma("gpsimd", wcat_d[hl, dc * 128:(dc + 1) * 128, :],
                            big4.ap()[:, hl * 8192 + dc * 1024:hl * 8192 + (dc + 1) * 1024], s="sG")

        tok_sp = [0, 0, 0, 0]
        last_apply = [0, 0, 0, 0]
        tr_cp = [0, 0]
        tr_done_a = [0, 0, 0, 0]
        tr_done_v = [0, 0, 0, 0]
        p1_tx = [0, 0, 0, 0]
        p1_tv = [0, 0, 0, 0]
        p1_ta = [0, 0, 0, 0]
        p1_tn = [0, 0, 0, 0]
        sqscr = v32.ap().bitcast(BF16)[:, 0:1024]

        def p1_xa(t):
            par = t & 3
            arena = stage if par < 2 else lnt
            q_ = par & 1
            return arena.ap()[:, q_ * 1024:(q_ + 1) * 1024]

        def p1_la(t):
            par = t & 3
            arena = stage if par < 2 else lnt
            q_ = par & 1
            return arena.ap()[:, 2048 + q_ * 1024:2048 + (q_ + 1) * 1024]

        def p1_st(t, j):
            par = t & 3
            return stats.ap()[:, par * 8 + j:par * 8 + j + 1]

        def st_load(t):
            par = t & 3
            wait("gpsimd", "sA", last_apply[par])
            p1_tx[par] = dma("gpsimd", p1_xa(t), x_in[t * 128:(t + 1) * 128, :], s="sG")

        def st_stats(t):
            par = t & 3
            xa = p1_xa(t)
            sm, sq, mn, vr = p1_st(t, 0), p1_st(t, 1), p1_st(t, 2), p1_st(t, 3)
            wait("vector", "sG", p1_tx[par])
            run("vector", lambda e, o=sm, i=xa: e.reduce_sum(o, i, axis=AX.X))
            tst = run("vector", lambda e, i=xa, q=sq: e.scalar_tensor_tensor(sqscr, i, 1.0, i, ALU.mult, ALU.mult, accum_out=q), s="sV")
            wait("vector", "sV", tst)
            runw("vector", lambda e, m=mn, s_=sm: e.tensor_scalar_mul(m, s_, 1.0 / 1024.0))
            runw("vector", lambda e, v=vr, q=sq: e.tensor_scalar_mul(v, q, 1.0 / 1024.0))
            runw("vector", lambda e, m=mn: e.tensor_tensor(r32.ap()[:, 0:1], m, m, ALU.mult))
            p1_tv[par] = runw("vector", lambda e, v=vr: e.tensor_tensor(v, v, r32.ap()[:, 0:1], ALU.subtract))
            if DBG_DUMP:
                run("vector", lambda e, t=t, s_=stats.ap()[:, par * 8:par * 8 + 6]:
                    e.tensor_copy(sttrace.ap()[:, t * 8:t * 8 + 6], s_))

        def st_lnexp(t):
            par = t & 3
            vr, rs = p1_st(t, 3), p1_st(t, 4)
            wait("scalar", "sV", p1_tv[par])
            _tl = run("scalar", lambda e, o=rs, v=vr: e.activation(o, v, AF.Ln, bias=eps_t.ap()[:, 0:1]), s="sA")
            wait("scalar", "sA", _tl)
            p1_ta[par] = run("scalar", lambda e, o=rs: e.activation(o, o, AF.Exp, scale=-0.5), s="sA")
            wait("scalar", "sA", p1_ta[par])

        def st_tn(t):
            par = t & 3
            mn, rs, nm = p1_st(t, 2), p1_st(t, 4), p1_st(t, 5)
            wait("vector", "sA", p1_ta[par])
            p1_tn[par] = runw("vector", lambda e, n=nm, m=mn, r=rs: e.tensor_scalar(n, m, r, -1.0, ALU.mult, ALU.mult))

        def st_apply(t):
            par = t & 3
            rs, nm = p1_st(t, 4), p1_st(t, 5)
            wait("scalar", "sV", p1_tn[par])
            wait("scalar", "sG", tok_sp[par])
            last_apply[par] = run("scalar", lambda e, o=p1_la(t), i=p1_xa(t), r=rs, n=nm:
                                  e.activation(o, i, AF.Identity, bias=n, scale=r), s="sA")

        def st_tr(t):
            par = t & 3
            la = p1_la(t)
            trbank = (ps_p.ap().bitcast(BF16)[:, 0:128], ps_m.ap().bitcast(BF16)[:, 0:128])
            wait("tensor", "sA", last_apply[par])
            if t == 0:
                wait("tensor", "sL", t_id)
            for dc in range(8):
                b = dc & 1
                wait("tensor", "sA" if b == 0 else "sV", tr_cp[b])
                ttr = run("tensor", lambda e, o=trbank[b], i=la[:, dc * 128:(dc + 1) * 128]:
                          e.transpose(o, i, ident_s[:]), s="sP")
                if b == 0:
                    wait("scalar", "sP", ttr)
                    tr_cp[0] = run("scalar", lambda e, o=la[:, dc * 128:(dc + 1) * 128], i=trbank[0]:
                                   e.copy(o, i), s="sA")
                else:
                    wait("vector", "sP", ttr)
                    tr_cp[1] = run("vector", lambda e, o=la[:, dc * 128:(dc + 1) * 128], i=trbank[1]:
                                   e.tensor_copy(o, i), s="sV")
            tr_done_a[par] = tr_cp[0]
            tr_done_v[par] = tr_cp[1]

        def st_spill(t):
            par = t & 3
            wait("gpsimd", "sA", tr_done_a[par])
            wait("gpsimd", "sV", tr_done_v[par])
            dst = ln1T_d.ap().rearrange("(dc p) t -> p dc t", p=128)[:, :, t * 128:(t + 1) * 128]
            tok_sp[par] = dma("gpsimd", dst, p1_la(t).rearrange("p (dc t) -> p dc t", t=128), s="sG")

        P1_PIPE = ((6, st_spill), (4, st_apply), (2, st_lnexp),
                   (1, st_stats), (3, st_tn), (5, st_tr), (0, st_load))
        if "ph1" in DBG_PHASES:
            for step in range(NT1 + 6):
                for k, fn in P1_PIPE:
                    t = step - k
                    if 0 <= t < NT1:
                        fn(t)

        barrier()
        if DBG_DUMP:
            dd0 = dma("gpsimd", dmp_ln2[:, :], ln1T_d[:, :], s="sG")
            wait("gpsimd", "sG", dd0)
            barrier()

        kready = [[0] * NTC for _ in range(NPAIR)]
        vready = [[0] * NTC for _ in range(NPAIR)]
        qready = [0] * NPAIR
        bready = [0] * NPAIR
        ks_free = [0, 0]
        vs_free = [0, 0]
        qs_free = [0, 0]
        lnt_free = [0, 0]
        cs_free = [0, 0]
        wp_free = [0, 0]
        biasp_free = [0, 0]
        PS = {"psp": 0, "psm": 0}
        qrope = [0] * NPAIR

        def lnslice(tpar, dc):
            return lnt.ap()[:, tpar * 4096 + dc * 512:tpar * 4096 + (dc + 1) * 512]

        def wslice(par, j, dc):
            o = par * 3072 + j * 1024 + dc * 128
            return wpair.ap()[:, o:o + 128]

        def emit_proj_unit(p, tc, bites=None):
            par = p & 1
            tpar = tc & 1
            kslot = big4.ap()[:, par * S + tc * 512:par * S + (tc + 1) * 512]
            vslot = vs.ap()[:, par * S + tc * 512:par * S + (tc + 1) * 512]
            qslot = qs.ap()[:, par * TQ + tc * 512:par * TQ + (tc + 1) * 512] if tc < 4 else None
            U = {}

            def sec(fn):
                if bites is None:
                    fn()
                else:
                    bites.append(fn)

            def s_dma():
                if tc == 0:
                    wait("sync", "sP", wp_free[par])
                    for j, w in enumerate((wkT, wqT, wvT)):
                        for dc in range(8):
                            PS[f"w{par}"] = dma("sync", wslice(par, j, dc),
                                                w[dc * 128:(dc + 1) * 128, p * 128:(p + 1) * 128])
                wait("sync", "sP", lnt_free[tpar])
                for dc in range(8):
                    PS[f"ln{tpar}"] = dma("sync", lnslice(tpar, dc),
                                          ln1T_d[dc * 128:(dc + 1) * 128, tc * 512:(tc + 1) * 512])
                wait("sync", "sV", cs_free[tpar])
                dma("sync", cossin.ap()[:, tpar * 1024:tpar * 1024 + 512], cosT[:, tc * 512:(tc + 1) * 512])
                U["t_cs"] = dma("sync", cossin.ap()[:, tpar * 1024 + 512:tpar * 1024 + 1024],
                                sinT[:, tc * 512:(tc + 1) * 512])
                U["t_ln"] = PS[f"ln{tpar}"]
                U["t_w"] = PS[f"w{par}"]
            sec(s_dma)

            def s_k():
                wait("tensor", "sL", max(U["t_ln"], U["t_w"]))
                if tc == 0:
                    wait("tensor", "sP", ks_free[par])
                    wait("tensor", "sP", vs_free[par])
                    wait("tensor", "sP", qs_free[par])
                wait("tensor", "sV", PS["psp"])
                tkmm = None
                for dc in range(8):
                    tkmm = run("tensor", lambda e, a=wslice(par, 0, dc), b=lnslice(tpar, dc), st=(dc == 0), sp=(dc == 7):
                               e.matmul(ps_p[:], a, b, start=st, stop=sp), s="sP" if dc == 7 else None)
                wait("vector", "sP", tkmm)
                U["tkev"] = run("vector", lambda e, o=kslot: e.tensor_scalar(o, ps_p[:], bk32.ap()[:, p:p + 1],
                                                                            None, ALU.add), s="sV")
                PS["psp"] = U["tkev"]
            sec(s_k)

            def s_v(g0):
                vev = U["vev"]
                for g in (g0, g0 + 1):
                    bank = ps_m if (g & 1) == 0 else ps_p
                    wait("tensor", "sV", vev[g & 1])
                    if g == 0:
                        wait("tensor", "sA", PS.get("psm_scalar", 0))
                    for dc in range(8):
                        run("tensor", lambda e, a=lnslice(tpar, dc)[:, g * 128:(g + 1) * 128], b=wslice(par, 2, dc),
                            st=(dc == 0), bk=bank: e.matmul(bk[:, 0:128], a, b, start=st, stop=False))
                    tvmm = run("tensor", lambda e, bk=bank: e.matmul(bk[:, 0:128], ones[0:1, 0:128],
                                                                    bv_s[0:1, p * 128:(p + 1) * 128],
                                                                    start=False, stop=True), s="sP")
                    wait("vector", "sP", tvmm)
                    vev[g & 1] = run("vector", lambda e, o=vslot[:, g * 128:(g + 1) * 128], bk=bank:
                                     e.tensor_copy(o, bk[:, 0:128]), s="sV")

            def s_v0():
                U["vev"] = [PS["psm"], U["tkev"]]
                s_v(0)
            sec(s_v0)

            def s_v2():
                s_v(2)
                vready[p][tc] = max(U["vev"])
                PS["psm"] = U["vev"][0]
                PS["psp"] = U["vev"][1]
            sec(s_v2)

            def s_q():
                if tc >= 4:
                    return
                wait("tensor", "sV", PS["psp"])
                tqmm = None
                for dc in range(8):
                    tqmm = run("tensor", lambda e, a=wslice(par, 1, dc), b=lnslice(tpar, dc), st=(dc == 0), sp=(dc == 7):
                               e.matmul(ps_p[:], a, b, start=st, stop=sp), s="sP" if dc == 7 else None)
                wait("vector", "sP", tqmm)
                U["tqev"] = run("vector", lambda e, o=qslot: e.tensor_scalar(o, ps_p[:], bq32.ap()[:, p:p + 1],
                                                                            LOG2E / 8.0, ALU.add, ALU.mult), s="sV")
                PS["psp"] = U["tqev"]
            sec(s_q)

            jnkb = stage.ap()[:, 0:512]

            def s_norm(slot_name, base):
                slot = kslot if slot_name == "k" else qslot
                if slot is None:
                    return
                tok_ev = U["tkev"] if slot_name == "k" else U.get("tqev", 0)
                wait("vector", "sV", tok_ev)
                wait("vector", "sP", PS.get("normmm", 0))
                tsq = run("vector", lambda e, j=jnkb, i=slot: e.tensor_tensor(j, i, i, ALU.mult), s="sV")
                wait("vector", "sV", tsq)
                tsq = run("vector", lambda e: e.tensor_copy(m0buf.ap()[0:1, 15:16], m0buf.ap()[0:1, 14:15]), s="sV")
                for h in range(2):
                    wait("tensor", "sV", tsq)
                    wait("tensor", "sV", PS["psm"])
                    wait("tensor", "sA", PS.get("psm_scalar", 0))
                    tnq = run("tensor", lambda e, h=h: e.matmul(ps_m[0:1, :], hsel_s[:, h:h + 1], jnkb,
                                                                start=True, stop=True), s="sP")
                    PS["normmm"] = tnq
                    wait("vector", "sP", tnq)
                    runw("vector", lambda e: e.reduce_max(m0buf.ap()[0:1, 14:15], ps_m[0:1, :], axis=AX.X))
                    mslot = m0buf.ap()[0:1, par * 6 + base + h:par * 6 + base + h + 1]
                    PS["psm"] = runw("vector", lambda e, m=mslot: e.tensor_tensor(
                        m, m, m0buf.ap()[0:1, 14:15], ALU.max))
            sec(lambda: s_norm("k", 0))
            sec(lambda: s_norm("q", 2))

            def s_rope(slot_name):
                slot = kslot if slot_name == "k" else qslot
                if slot is None:
                    return
                cosap = cossin.ap()[:, tpar * 1024:tpar * 1024 + 512]
                sinap = cossin.ap()[:, tpar * 1024 + 512:tpar * 1024 + 1024]
                if tc == 0 and slot_name == "k":
                    wait("tensor", "sL", t_rot)
                wait("tensor", "sL", U["t_cs"])
                wait("tensor", "sV", PS["psm"])
                wait("tensor", "sA", PS.get("psm_scalar", 0))
                trot = run("tensor", lambda e, s_=slot: e.matmul(ps_m[:], rot_s[:], s_, start=True, stop=True), s="sP")
                j1 = stage.ap()[:, 512:1024]
                j2 = stage.ap()[:, 1024:1536]
                wait("vector", "sP", trot)
                run("vector", lambda e, j=j1, s_=slot, c=cosap: e.tensor_tensor(j, s_, c, ALU.mult))
                tm = run("vector", lambda e, j=j2, s_=sinap: e.tensor_tensor(j, ps_m[:], s_, ALU.mult), s="sV")
                PS["psm"] = tm
                tfin = run("vector", lambda e, o=slot, a=j1, b=j2: e.tensor_tensor(o, a, b, ALU.add), s="sV")
                if slot_name == "k":
                    kready[p][tc] = tfin
                else:
                    qrope[p] = max(qrope[p], tfin)
            sec(lambda: s_rope("k"))
            sec(lambda: s_rope("q"))

            def s_tail():
                lnt_free[tpar] = run("tensor", lambda e: e.matmul(ps_den[64:65, 0:1], ones[0:1, 0:1],
                                                                 ones[0:1, 0:1], start=True, stop=True), s="sP")
                cs_free[tpar] = max(kready[p][tc], qrope[p] if tc < 4 else 0)
                if tc == NTC - 1:
                    qready[p] = qrope[p]
                    wp_free[par] = lnt_free[tpar]
                    bp = par
                    runw("vector", lambda e: e.tensor_tensor(m0buf.ap()[0:1, 12:13],
                                                             m0buf.ap()[0:1, bp * 6 + 0:bp * 6 + 1],
                                                             m0buf.ap()[0:1, bp * 6 + 2:bp * 6 + 3], ALU.mult))
                    runw("vector", lambda e: e.tensor_tensor(m0buf.ap()[0:1, 13:14],
                                                             m0buf.ap()[0:1, bp * 6 + 1:bp * 6 + 2],
                                                             m0buf.ap()[0:1, bp * 6 + 3:bp * 6 + 4], ALU.mult))
                    runw("vector", lambda e: e.tensor_tensor(m0buf.ap()[0:1, 12:13], m0buf.ap()[0:1, 12:13],
                                                             m0buf.ap()[0:1, 13:14], ALU.max))
                    tz = run("vector", lambda e: e.memset(m0buf.ap()[0:1, bp * 6:bp * 6 + 4], 0.0), s="sV")
                    wait("scalar", "sV", tz)
                    _tl = run("scalar", lambda e: e.activation(m0buf.ap()[0:1, 13:14], m0buf.ap()[0:1, 12:13],
                                                               AF.Ln), s="sA")
                    wait("scalar", "sA", _tl)
                    tm0 = run("scalar", lambda e: e.activation(m0buf.ap()[0:1, 13:14], m0buf.ap()[0:1, 13:14],
                                                               AF.Exp, scale=0.5), s="sA")
                    wait("vector", "sA", tm0)
                    tmb = run("vector", lambda e, b=bp: e.tensor_copy(m0bf.ap()[0:1, b:b + 1],
                                                                      m0buf.ap()[0:1, 13:14]), s="sV")
                    wait("tensor", "sV", tmb)
                    wait("tensor", "sV", PS["psm"])
                    tmm = run("tensor", lambda e, b=bp: e.matmul(ps_m[:, 0:1], ones[0:1, 0:128],
                                                                 m0bf.ap()[0:1, b:b + 1], start=True, stop=True), s="sP")
                    wait("scalar", "sP", tmm)
                    wait("scalar", "sA", biasp_free[bp])
                    bready[p] = run("scalar", lambda e, b=bp: e.activation(biasp.ap()[:, b:b + 1], ps_m[:, 0:1],
                                                                          AF.Identity, scale=-LN2), s="sA")
                    PS["psm_scalar"] = bready[p]
            sec(s_tail)

        ik = [0]
        act_done = []
        av_done = []
        epi_tok = [0]

        pp_free = [0, 0, 0, 0]
        epi_act = [0]

        pre_scored = {}

        def emit_scores(p_, qb_, kt_, i):
            par_ = p_ & 1
            stX = ps_a if (i & 1) == 0 else ps_b
            tc = (kt_ * 128) // 512
            wait("tensor", "sV", kready[p_][tc])
            if qb_ == 0 and kt_ == 0:
                wait("tensor", "sV", qready[p_])
            if i >= 2:
                wait("tensor", "sA", act_done[i - 2])
            kofs = par_ * S + kt_ * 128
            qofs = par_ * TQ + qb_ * 512
            run("tensor", lambda e, st=stX, ko=kofs, qo=qofs:
                e.matmul(st[:, 0:512], big4.ap()[0:64, ko:ko + 128], qs.ap()[0:64, qo:qo + 512],
                         start=True, stop=True))
            tsc = run("tensor", lambda e, st=stX, ko=kofs, qo=qofs:
                      e.matmul(st[:, 512:1024], big4.ap()[64:128, ko:ko + 128],
                               qs.ap()[64:128, qo:qo + 512], start=True, stop=True), s="sP")
            return tsc, stX

        def emit_attention(p, units, pnext=None):
            par = p & 1
            for qb in range(NQB):
                exp_toks = [0] * NKT

                def emit_avden(kt):
                    pslot = pp.ap()[:, (kt & 3) * 1024:(kt & 3) * 1024 + 1024]
                    tc = (kt * 128) // 512
                    wait("tensor", "sA", exp_toks[kt])
                    wait("tensor", "sV", vready[p][tc])
                    if kt == 0:
                        wait("tensor", "sV", epi_tok[0])
                        wait("tensor", "sA", epi_act[0])
                    st0, sp0 = (kt == 0), (kt == NKT - 1)
                    vofs = par * S + kt * 128
                    run("tensor", lambda e, vo=vofs, pq=pslot, st=st0, sp=sp0:
                        e.matmul(ps_av[0:64, :], vs.ap()[:, vo:vo + 64], pq[:, 0:512], start=st, stop=sp,
                                 tile_position=(0, 0)))
                    run("tensor", lambda e, vo=vofs, pq=pslot, st=st0, sp=sp0:
                        e.matmul(ps_av[64:128, :], vs.ap()[:, vo + 64:vo + 128], pq[:, 512:1024], start=st, stop=sp,
                                 tile_position=(0, 64)))
                    run("tensor", lambda e, pq=pslot, st=st0, sp=sp0:
                        e.matmul(ps_den[0:1, :], ones[:, 0:1], pq[:, 0:512], start=st, stop=sp,
                                 tile_position=(0, 0)))
                    tav = run("tensor", lambda e, pq=pslot, st=st0, sp=sp0:
                              e.matmul(ps_den[32:33, :], ones[:, 0:1], pq[:, 512:1024], start=st, stop=sp,
                                       tile_position=(0, 32)), s="sP")
                    av_done.append(tav)
                    pp_free[kt & 3] = tav

                for kt in range(NKT):
                    i = ik[0]
                    ik[0] += 1
                    if (p, qb, kt) in pre_scored:
                        tsc, stX = pre_scored.pop((p, qb, kt))
                    else:
                        tsc, stX = emit_scores(p, qb, kt, i)
                    wait("scalar", "sP", tsc)
                    if qb == 0 and kt == 0:
                        wait("scalar", "sA", bready[p])
                    wait("scalar", "sP", pp_free[kt & 3])
                    pslot = pp.ap()[:, (kt & 3) * 1024:(kt & 3) * 1024 + 1024]
                    texp = run("scalar", lambda e, o=pslot, st=stX, b=par:
                               e.activation(o, st[:], AF.Exp, bias=biasp.ap()[:, b:b + 1], scale=LN2), s="sA")
                    act_done.append(texp)
                    exp_toks[kt] = texp
                    if kt >= 2:
                        emit_avden(kt - 2)
                    if units is not None:
                        idx = qb * NKT + kt
                        want = min(len(units), (idx + 1) * len(units) // (NQB * NKT - 32))
                        while spliced[0] < want:
                            units[spliced[0]]()
                            spliced[0] += 1
                emit_avden(NKT - 2)
                emit_avden(NKT - 1)
                np_, nqb = (p, qb + 1) if qb + 1 < NQB else (pnext, 0)
                if np_ is not None:
                    for j in range(2):
                        pre_scored[(np_, nqb, j)] = emit_scores(np_, nqb, j, ik[0] + j)
                wait("scalar", "sP", av_done[-1])
                _t1 = run("scalar", lambda e: e.activation(r32.ap()[0:33, 0:512], ps_den[0:33, :], AF.Ln), s="sA")
                wait("scalar", "sA", _t1)
                tre = run("scalar", lambda e: e.activation(r32.ap()[0:33, 0:512], r32.ap()[0:33, 0:512],
                                                           AF.Exp, scale=-1.0), s="sA")
                epi_act[0] = tre
                wait("vector", "sP", av_done[-1])
                tcv = run("vector", lambda e: e.tensor_copy(av32[:], ps_av[:]), s="sV")
                rb1 = stage.ap()[0:1, 2048:2560]
                rb2 = stage.ap()[32:33, 2048:2560]
                wait("vector", "sA", tre)
                trc = runw("vector", lambda e: e.tensor_copy(stage.ap()[0:33, 2048:2560], r32.ap()[0:33, 0:512]))
                wait("tensor", "sV", trc)
                run("tensor", lambda e: e.matmul(ps_av[0:64, :], ones[0:1, 0:64], rb1, start=True, stop=True,
                                                 tile_position=(0, 0)))
                tbc = run("tensor", lambda e: e.matmul(ps_av[64:128, :], ones[32:33, 0:64], rb2,
                                                       start=True, stop=True, tile_position=(32, 64)), s="sP")
                wait("vector", "sP", tbc)
                aslot = attnT.ap()[:, p * TQ + qb * 512:p * TQ + (qb + 1) * 512]
                epi_tok[0] = run("vector", lambda e, o=aslot: e.tensor_tensor(o, av32[:], ps_av[:], ALU.mult), s="sV")
            ks_free[par] = av_done[-1]
            vs_free[par] = av_done[-1]
            qs_free[par] = av_done[-1]
            biasp_free[par] = act_done[-1]

        spliced = [0]
        if "pairs" in DBG_PHASES:
            for tc in range(NTC if DBG_NPAIR >= 0 else -DBG_NPAIR):
                emit_proj_unit(0, tc)
            for p in range(max(DBG_NPAIR, 0)):
                if p + 1 < DBG_NPAIR:
                    units = []
                    for tc in range(NTC):
                        emit_proj_unit(p + 1, tc, bites=units)
                else:
                    units = None
                spliced[0] = 0
                emit_attention(p, units, pnext=p + 1 if p + 1 < DBG_NPAIR else None)

        barrier()
        if DBG_DUMP:
            dd = 0
            dd = dma("gpsimd", dmp_ln[:, :], ln1T_d[:, :], s="sG")
            dd = dma("gpsimd", dmp_ks[:, :], big4.ap()[:, :], s="sG")
            dd = dma("gpsimd", dmp_vs[:, :], vs.ap()[:, :], s="sG")
            dd = dma("gpsimd", dmp_qs[:, :], qs.ap()[:, :], s="sG")
            dd = dma("gpsimd", dmp_at[:, :], attnT.ap()[:, :], s="sG")
            dd = dma("gpsimd", dmp_bp[:, :], biasp.ap()[:, :], s="sG")
            dd = dma("gpsimd", dmp_m0[:, :], m0buf.ap()[:, :], s="sG")
            dd = dma("gpsimd", dmp_st[:, :], sttrace.ap()[:, :], s="sG")
            for hl in range(2):
                for dc in range(8):
                    dd = dma("gpsimd", dmp_wc[:, (hl * 8 + dc) * 1024:(hl * 8 + dc + 1) * 1024],
                             wcat_d[hl, dc * 128:(dc + 1) * 128, :], s="sG")
            wait("gpsimd", "sG", dd)
            barrier()

        if "ph3" not in DBG_PHASES:
            tz0 = run("vector", lambda e: e.memset(outb[:], 0.0), s="sV")
            wait("gpsimd", "sV", tz0)
            td0 = 0
            for t in range(NT3):
                td0 = dma("gpsimd", out_d[t * 128:(t + 1) * 128, :], outb.ap()[:, 0:1024], s="sG")
            wait("gpsimd", "sG", td0)
            wait("gpsimd", "sG", t_wcd)
            _finish(block, ops)
            return nc
        t_wo = 0
        for dc in range(8):
            t_wo = dma("sync", warena.ap()[:, dc * 1024:(dc + 1) * 1024], woT[dc * 128:(dc + 1) * 128, :])
        t_wcl = 0
        for hl in range(2):
            for dc in range(8):
                t_wcl = dma("sync", big4.ap()[:, hl * 8192 + dc * 1024:hl * 8192 + (dc + 1) * 1024],
                            wcat_d[hl, dc * 128:(dc + 1) * 128, :])
        p3_x0 = [0, 0]
        p3_adone = [0, 0]
        p3_v3free = [0, 0]
        p3_w3free = [0, 0]
        p3_psa = [0, 0]
        p3_psb = [0]
        p3_tv2 = [0, 0]
        p3_tF = [0, 0]
        tr3_cp = [0, 0]
        tr3_last = [0]
        outdma = [0, 0]
        p3_sqscr = pp.ap()[:, 0:1024]
        p3_jnkb = pp.ap()[:, 1024:2048]

        def p3_x1(t):
            return vs.ap()[:, t * 1024:(t + 1) * 1024]

        def p3_slot(par, j):
            return stats.ap()[:, par * 16 + j:par * 16 + j + 1]

        p3_tO = [0, 0]

        def p3A1(t):
            par = t & 1
            x0a = stage.ap()[:, par * 1024:(par + 1) * 1024]
            wait("gpsimd", "sV", p3_adone[par])
            p3_x0[par] = dma("gpsimd", x0a, x_in[t * 128:(t + 1) * 128, :], s="sG")
            if t == 0:
                wait("tensor", "sL", t_wo)
            psout = (ps_a[:, 0:512], ps_a[:, 512:1024]) if par == 0 \
                else (ps_av.ap()[:, :], ps_den.ap()[:, :])
            wait("tensor", "sA", p3_psa[par])
            for ns in range(2):
                for fc in range(8):
                    run("tensor", lambda e, fc=fc, ns=ns, t=t:
                        e.matmul(psout[ns],
                                 attnT.ap()[:, fc * TQ + t * 128:fc * TQ + (t + 1) * 128],
                                 warena.ap()[:, fc * 1024 + ns * 512:fc * 1024 + (ns + 1) * 512],
                                 start=(fc == 0), stop=False))
                p3_tO[par] = run("tensor", lambda e, ns=ns: e.matmul(psout[ns], ones[0:1, 0:128],
                                                                    bo_s[0:1, ns * 512:(ns + 1) * 512], start=False, stop=True),
                                 s="sP" if ns == 1 else None)

        def p3A2(t):
            par = t & 1
            x0a = stage.ap()[:, par * 1024:(par + 1) * 1024]
            x1a = p3_x1(t)
            v3 = v32.ap()[:, par * 1024:(par + 1) * 1024]
            sm, sq, mn, vr = (p3_slot(par, j) for j in range(4))
            rs, nm = p3_slot(par, 4), p3_slot(par, 5)
            wait("scalar", "sP", p3_tO[par])
            wait("scalar", "sV", p3_v3free[par])
            if par == 0:
                oc = run("scalar", lambda e, o=v3, s_=sm: e.activation(o, ps_a[:], AF.Copy, accum_out=s_), s="sA")
            else:
                run("scalar", lambda e, o=v3[:, 0:512], s_=sm: e.activation(o, ps_av[:], AF.Copy, accum_out=s_), s="sA")
                oc = run("scalar", lambda e, o=v3[:, 512:1024], s_=m0buf.ap()[:, par:par + 1]:
                         e.activation(o, ps_den[:], AF.Copy, accum_out=s_), s="sA")
            p3_psa[par] = oc
            wait("vector", "sA", oc)
            _ts = run("vector", lambda e, i=v3, q=sq: e.scalar_tensor_tensor(p3_sqscr, i, 1.0, i, ALU.mult, ALU.mult, accum_out=q), s="sV")
            wait("vector", "sV", _ts)
            if par == 1:
                runw("vector", lambda e, s_=sm: e.tensor_tensor(s_, s_, m0buf.ap()[:, par:par + 1], ALU.add))
            runw("vector", lambda e, m=mn, s_=sm: e.tensor_scalar_mul(m, s_, 1.0 / 1024.0))
            runw("vector", lambda e, v=vr, q=sq: e.tensor_scalar_mul(v, q, 1.0 / 1024.0))
            runw("vector", lambda e, m=mn: e.tensor_tensor(r32.ap()[:, 0:1], m, m, ALU.mult))
            tv = runw("vector", lambda e, v=vr: e.tensor_tensor(v, v, r32.ap()[:, 0:1], ALU.subtract))
            wait("scalar", "sV", tv)
            _tl = run("scalar", lambda e, o=rs, v=vr: e.activation(o, v, AF.Ln, bias=eps_t.ap()[:, 0:1]), s="sA")
            wait("scalar", "sA", _tl)
            ta = run("scalar", lambda e, o=rs: e.activation(o, o, AF.Exp, scale=-0.5), s="sA")
            wait("scalar", "sA", ta)
            wait("vector", "sA", ta)
            runw("vector", lambda e, n=nm, m=mn, r=rs: e.tensor_scalar(n, m, r, -1.0, ALU.mult, ALU.mult))
            wait("vector", "sG", p3_x0[par])
            tvf = runw("vector", lambda e, i=v3, r=rs, n=nm: e.tensor_scalar(p3_jnkb, i, r, n, ALU.mult, ALU.add))
            p3_v3free[par] = tvf
            tx1 = run("vector", lambda e, o=x1a, x0=x0a: e.tensor_tensor(o, p3_jnkb, x0, ALU.add), s="sV")
            p3_adone[par] = tx1
            sm2, sq2, mn2, vr2 = (p3_slot(par, 6 + j) for j in range(4))
            wait("vector", "sV", tx1)
            run("vector", lambda e, o=sm2, i=x1a: e.reduce_sum(o, i, axis=AX.X))
            _ts = run("vector", lambda e, i=x1a, q=sq2: e.scalar_tensor_tensor(p3_sqscr, i, 1.0, i, ALU.mult, ALU.mult, accum_out=q), s="sV")
            wait("vector", "sV", _ts)
            runw("vector", lambda e, m=mn2, s_=sm2: e.tensor_scalar_mul(m, s_, 1.0 / 1024.0))
            runw("vector", lambda e, v=vr2, q=sq2: e.tensor_scalar_mul(v, q, 1.0 / 1024.0))
            runw("vector", lambda e, m=mn2: e.tensor_tensor(r32.ap()[:, 0:1], m, m, ALU.mult))
            p3_tv2[par] = runw("vector", lambda e, v=vr2: e.tensor_tensor(v, v, r32.ap()[:, 0:1], ALU.subtract))

        def p3B(t):
            par = t & 1
            x1a = p3_x1(t)
            ln2a = stage.ap()[:, 2048 + par * 1024:2048 + (par + 1) * 1024]
            ln2t = lnt.ap()[:, 0:1024]
            mn2, vr2 = p3_slot(par, 8), p3_slot(par, 9)
            rs2, nm2 = p3_slot(par, 10), p3_slot(par, 11)
            wait("scalar", "sV", p3_tv2[par])
            _tl = run("scalar", lambda e, o=rs2, v=vr2: e.activation(o, v, AF.Ln, bias=eps_t.ap()[:, 0:1]), s="sA")
            wait("scalar", "sA", _tl)
            ta2 = run("scalar", lambda e, o=rs2: e.activation(o, o, AF.Exp, scale=-0.5), s="sA")
            wait("scalar", "sA", ta2)
            wait("vector", "sA", ta2)
            tn2 = runw("vector", lambda e, n=nm2, m=mn2, r=rs2: e.tensor_scalar(n, m, r, -1.0, ALU.mult, ALU.mult))
            wait("scalar", "sV", tn2)
            wait("scalar", "sP", tr3_last[0])
            tl2 = run("scalar", lambda e, o=ln2a, i=x1a, r=rs2, n=nm2: e.activation(o, i, AF.Identity, bias=n, scale=r), s="sA")
            trbank = (ps_p.ap().bitcast(BF16)[:, 0:128], ps_m.ap().bitcast(BF16)[:, 0:128])
            wait("tensor", "sA", tl2)
            wait("scalar", "sP", p3_tF[1 - par])
            wait("vector", "sP", p3_tF[1 - par])
            for dc in range(8):
                b = dc & 1
                wait("tensor", "sA" if b == 0 else "sV", tr3_cp[b])
                ttr = run("tensor", lambda e, o=trbank[b], i=ln2a[:, dc * 128:(dc + 1) * 128]:
                          e.transpose(o, i, ident_s[:]), s="sP")
                if b == 0:
                    wait("scalar", "sP", ttr)
                    tr3_cp[0] = run("scalar", lambda e, o=ln2t[:, dc * 128:(dc + 1) * 128], i=trbank[0]:
                                    e.copy(o, i), s="sA")
                else:
                    wait("vector", "sP", ttr)
                    tr3_cp[1] = run("vector", lambda e, o=ln2t[:, dc * 128:(dc + 1) * 128], i=trbank[1]:
                                    e.tensor_copy(o, i), s="sV")
                tr3_last[0] = ttr
            wait("tensor", "sA", tr3_cp[0])
            wait("tensor", "sV", tr3_cp[1])
            if t == 0:
                wait("tensor", "sL", t_wcl)
                wait("tensor", "sV", t_fbl)
            wait("tensor", "sA", p3_psb[0])
            for ns in range(2):
                for hl in range(2):
                    for dc in range(8):
                        run("tensor", lambda e, dc=dc, hl=hl, ns=ns:
                            e.matmul(ps_b[:, ns * 512:(ns + 1) * 512],
                                     ln2t[:, dc * 128:(dc + 1) * 128],
                                     big4.ap()[:, hl * 8192 + dc * 1024 + ns * 512:hl * 8192 + dc * 1024 + (ns + 1) * 512],
                                     start=(hl == 0 and dc == 0), stop=False))
                run("tensor", lambda e, ns=ns: e.matmul(ps_b[:, ns * 512:(ns + 1) * 512], ones[0:1, 0:128],
                                                        fbh[0:1, ns * 512:(ns + 1) * 512], start=False, stop=False))
                tF = run("tensor", lambda e, ns=ns: e.matmul(ps_b[:, ns * 512:(ns + 1) * 512], ones[0:1, 0:128],
                                                             fbl[0:1, ns * 512:(ns + 1) * 512], start=False, stop=True),
                         s="sP" if ns == 1 else None)
            p3_tF[par] = tF

        def p3C(t):
            par = t & 1
            x1a = p3_x1(t)
            w3 = qs.ap().bitcast(F32)[:, par * 1024:(par + 1) * 1024]
            sm3, sq3, mn3, vr3 = (p3_slot(par, 12 + j) for j in range(4))
            rs, nm = p3_slot(par, 4), p3_slot(par, 5)
            wait("scalar", "sP", p3_tF[par])
            wait("scalar", "sV", p3_w3free[par])
            tw3 = run("scalar", lambda e, o=w3, s_=sm3: e.activation(o, ps_b[:], AF.Copy, accum_out=s_), s="sA")
            p3_psb[0] = tw3
            wait("vector", "sA", tw3)
            _ts = run("vector", lambda e, i=w3, q=sq3: e.scalar_tensor_tensor(p3_sqscr, i, 1.0, i, ALU.mult, ALU.mult, accum_out=q), s="sV")
            wait("vector", "sV", _ts)
            runw("vector", lambda e, m=mn3, s_=sm3: e.tensor_scalar_mul(m, s_, 1.0 / 1024.0))
            runw("vector", lambda e, v=vr3, q=sq3: e.tensor_scalar_mul(v, q, 1.0 / 1024.0))
            runw("vector", lambda e, m=mn3: e.tensor_tensor(r32.ap()[:, 0:1], m, m, ALU.mult))
            tv3 = runw("vector", lambda e, v=vr3: e.tensor_tensor(v, v, r32.ap()[:, 0:1], ALU.subtract))
            wait("scalar", "sV", tv3)
            _tl = run("scalar", lambda e, o=rs, v=vr3: e.activation(o, v, AF.Ln, bias=eps_t.ap()[:, 0:1]), s="sA")
            wait("scalar", "sA", _tl)
            ta3 = run("scalar", lambda e, o=rs: e.activation(o, o, AF.Exp, scale=-0.5), s="sA")
            wait("scalar", "sA", ta3)
            wait("vector", "sA", ta3)
            runw("vector", lambda e, n=nm, m=mn3, r=rs: e.tensor_scalar(n, m, r, -1.0, ALU.mult, ALU.mult))
            outa = outb.ap()[:, par * 1024:(par + 1) * 1024]
            wait("vector", "sG", outdma[par])
            runw("vector", lambda e, j=w3, r=rs, n=nm: e.tensor_scalar(j, j, r, n, ALU.mult, ALU.add))
            tout = run("vector", lambda e, o=outa, j=w3, x1=x1a: e.tensor_tensor(o, j, x1, ALU.add), s="sV")
            p3_w3free[par] = tout
            wait("gpsimd", "sV", tout)
            outdma[par] = dma("gpsimd", out_d[t * 128:(t + 1) * 128, :], outa, s="sG")

        for step in range(NT3 + 2):
            if step < NT3:
                p3A1(step)
            if 0 <= step - 2 < NT3:
                p3C(step - 2)
            if 0 <= step - 1 < NT3:
                p3B(step - 1)
            if step < NT3:
                p3A2(step)

        wait("gpsimd", "sG", outdma[0])
        wait("gpsimd", "sG", outdma[1])
        wait("gpsimd", "sG", t_wcd)
        _finish(block, ops)
    return nc


def _finish(block, ops):
    @block.sync
    def _(e):
        for fn in ops["sync"]:
            fn(e)

    @block.gpsimd
    def _(e):
        for fn in ops["gpsimd"]:
            fn(e)

    @block.tensor
    def _(e):
        for fn in ops["tensor"]:
            fn(e)

    @block.scalar
    def _(e):
        for fn in ops["scalar"]:
            fn(e)

    @block.vector
    def _(e):
        for fn in ops["vector"]:
            fn(e)


def kernel(L0_attn_input_embeds, L0_attn_Wq, L0_attn_bq, L0_attn_Wk, L0_attn_bk,
           L0_attn_Wv, L0_attn_bv, L0_attn_Wo, L0_attn_bo, L0_ffn_G1_R, L0_ffn_G2_R,
           L0_ffn_G1_I, L0_ffn_G2_I, L0_ffn_bias_real, L0_ffn_bias_imag):
    x = np.asarray(L0_attn_input_embeds)
    wq = np.asarray(L0_attn_Wq); wk = np.asarray(L0_attn_Wk)
    wv = np.asarray(L0_attn_Wv); wo = np.asarray(L0_attn_Wo)
    bq = np.asarray(L0_attn_bq); bk = np.asarray(L0_attn_bk)
    bv = np.asarray(L0_attn_bv); bo = np.asarray(L0_attn_bo)
    g1r_ = np.asarray(L0_ffn_G1_R, dtype=np.float32)
    g2r_ = np.asarray(L0_ffn_G2_R, dtype=np.float32)
    g1i_ = np.asarray(L0_ffn_G1_I, dtype=np.float32)
    g2i_ = np.asarray(L0_ffn_G2_I, dtype=np.float32)
    br = np.asarray(L0_ffn_bias_real, dtype=np.float32)
    bi = np.asarray(L0_ffn_bias_imag, dtype=np.float32)

    shared = {
        "wqT": np.ascontiguousarray(wq.T), "wkT": np.ascontiguousarray(wk.T),
        "wvT": np.ascontiguousarray(wv.T), "woT": np.ascontiguousarray(wo.T),
        "bqT": np.ascontiguousarray(bq.reshape(8, 128).T),
        "bkT": np.ascontiguousarray(bk.reshape(8, 128).T),
        "bv": np.ascontiguousarray(bv.reshape(1, D)),
        "bo": np.ascontiguousarray(bo.reshape(1, D)),
        "fbias": np.ascontiguousarray(np.concatenate([br, bi]).reshape(1, D)),
        "g1r": np.ascontiguousarray(np.transpose(g1r_[0], (2, 1, 0))),
        "g1i": np.ascontiguousarray(np.transpose(g1i_[0], (2, 1, 0))),
        "g2r": np.ascontiguousarray(np.transpose(g2r_[:, :, 0, :], (2, 0, 1))),
        "g2i": np.ascontiguousarray(np.transpose(g2i_[:, :, 0, :], (2, 0, 1))),
        "identT": np.eye(128, dtype=bf16),
    }
    hs = np.zeros((128, 33), dtype=bf16)
    hs[0:64, 0] = 1.0
    hs[64:128, 1] = 1.0
    shared["hsel"] = hs
    R = np.zeros((128, 128), dtype=np.float32)
    for blk in range(2):
        o = blk * 64
        for d in range(32):
            R[o + d, o + d + 32] = -1.0
            R[o + d + 32, o + d] = 1.0
    shared["rotT"] = np.ascontiguousarray(R.T).astype(bf16)
    inv = (1.0 / (10000.0 ** (np.arange(0, 64, 2, dtype=np.float32) / 64.0))).astype(np.float32)
    tpos = np.arange(S, dtype=np.float32)
    fr = np.outer(inv, tpos)
    cos128 = np.tile(np.cos(fr).astype(bf16), (4, 1))
    sin128 = np.tile(np.sin(fr).astype(bf16), (4, 1))

    nc = build_program()
    in_maps = []
    for c in range(NCORE):
        beta, qt = c // 4, c % 4
        roll = qt * TQ
        m = dict(shared)
        m["x"] = np.ascontiguousarray(np.roll(x[beta], -roll, axis=0))
        m["cosT"] = np.ascontiguousarray(np.roll(cos128, -roll, axis=1))
        m["sinT"] = np.ascontiguousarray(np.roll(sin128, -roll, axis=1))
        in_maps.append(m)

    res = run_bass_kernel_spmd(nc, in_maps, core_ids=list(range(NCORE)))
    out = np.empty((B, S, D), dtype=np.float32)
    for c in range(NCORE):
        beta, qt = c // 4, c % 4
        out[beta, qt * TQ:(qt + 1) * TQ, :] = res.results[c]["out"]
    return out
